# version 03
# baseline (speedup 1.0000x reference)
import numpy as np

import concourse.bass as bass
import concourse.mybir as mybir
from concourse.bass_utils import run_bass_kernel_spmd

F32 = mybir.dt.float32
BF16 = mybir.dt.bfloat16
I32 = mybir.dt.int32
ALU = mybir.AluOpType
AF = mybir.ActivationFunctionType
AX = mybir.AxisListType

NCORE = 8
T = 8192
TC = T // NCORE
D = 2048
H = 8192
CH = 128
NCHUNK = TC // CH
PT = T - TC
NPT = PT // CH
HB = 512
NHB = H // HB
LN_EPS = 1e-5
RB = 384.0



def _materialize(nc, prog, semh, sem_names):
    import contextlib
    with contextlib.ExitStack() as stack:
        for n in sem_names:
            semh[n] = stack.enter_context(nc.semaphore(n))
        block = stack.enter_context(nc.Block())

        @block.gpsimd
        def _(e):
            for f in prog["gpsimd"]:
                f(e)

        @block.sync
        def _(e):
            for f in prog["sync"]:
                f(e)

        @block.scalar
        def _(e):
            for f in prog["scalar"]:
                f(e)

        @block.vector
        def _(e):
            for f in prog["vector"]:
                f(e)

        @block.tensor
        def _(e):
            for f in prog["tensor"]:
                f(e)

    return nc

def build_program(limit=99):
    import os
    nc = bass.Bass("TRN2", target_bir_lowering=False, detect_race_conditions=bool(int(os.environ.get("KRC", "0"))))

    dx = nc.dram_tensor("xc", [TC, D], F32, kind="ExternalInput")
    dxprev = nc.dram_tensor("xprev", [PT, D], F32, kind="ExternalInput")
    dwt = nc.dram_tensor("wt", [D, D], F32, kind="ExternalInput")
    dgwt = nc.dram_tensor("gwt", [D, H], F32, kind="ExternalInput")
    dvwt = nc.dram_tensor("vwt", [D, H], F32, kind="ExternalInput")
    dowt = nc.dram_tensor("owt", [H, D], F32, kind="ExternalInput")
    dlam = nc.dram_tensor("lam_logit", [1, 4], F32, kind="ExternalInput")
    deta = nc.dram_tensor("eta", [1, 4], F32, kind="ExternalInput")
    dgam = nc.dram_tensor("gam", [1, D], F32, kind="ExternalInput")
    dbet = nc.dram_tensor("bet", [1, D], F32, kind="ExternalInput")
    dgb = nc.dram_tensor("gb", [1, H], F32, kind="ExternalInput")
    dvb = nc.dram_tensor("vb", [1, H], F32, kind="ExternalInput")
    dob = nc.dram_tensor("ob", [1, D], F32, kind="ExternalInput")
    dones = nc.dram_tensor("ones1", [1, 128], F32, kind="ExternalInput")
    ddmat = nc.dram_tensor("dmat", [128, 128], F32, kind="ExternalInput")
    dltm = nc.dram_tensor("ltm", [128, 128], F32, kind="ExternalInput")
    didn = nc.dram_tensor("idn", [128, 128], F32, kind="ExternalInput")
    dip1 = nc.dram_tensor("iotap1", [4, 128], F32, kind="ExternalInput")
    direv = nc.dram_tensor("iotarev", [128, 1], F32, kind="ExternalInput")
    dc128 = nc.dram_tensor("c128", [4, 1], F32, kind="ExternalInput")
    dpexp = nc.dram_tensor("pexp", [128, NPT], F32, kind="ExternalInput")
    dpmask = nc.dram_tensor("pmask", [128, NPT], F32, kind="ExternalInput")

    dout = nc.dram_tensor("out", [TC, D], F32, kind="ExternalOutput")
    _sk = "ExternalOutput" if os.environ.get("KDBG") else "Internal"
    sxbf = nc.dram_tensor("s_xbf", [TC, D], BF16, kind=_sk)
    sxq = nc.dram_tensor("s_xq", [TC, D], BF16, kind=_sk)
    sgv = nc.dram_tensor("s_gv", [TC, H], BF16, kind=_sk)
    sgvq = nc.dram_tensor("s_gvq", [TC, H], BF16, kind=_sk)
    sx1 = nc.dram_tensor("s_x1", [TC, D], F32, kind=_sk)

    prog = {"tensor": [], "vector": [], "scalar": [], "gpsimd": [], "sync": []}
    from collections import defaultdict
    sems = defaultdict(int)
    semh = {}
    sem_names = []

    def _sem(name):
        if name not in sem_names:
            sem_names.append(name)
        return name

    auto_cnt = {"vector": 0, "scalar": 0}

    def op(eng, fn):
        if eng in auto_cnt:
            k = auto_cnt[eng]
            s = "AV" if eng == "vector" else "AA"
            _sem(s)
            if k > 0:
                prog[eng].append(lambda e, s=s, k=k: e.wait_ge(semh[s], k))

            def chained(e, f=fn, s=s):
                f(e).then_inc(semh[s], 1)

            prog[eng].append(chained)
            auto_cnt[eng] = k + 1
        else:
            prog[eng].append(fn)

    def op_raw(eng, fn):
        prog[eng].append(fn)

    COMPUTE_SEMS = {"ST", "SV", "SA", "AV", "AA"}

    def wait(eng, sem, val):
        _sem(sem)
        if sem not in COMPUTE_SEMS:
            val = max(val, sems[sem]) if val > 0 else val
        if val > 0:
            op_raw(eng, lambda e, s=sem, v=val: e.wait_ge(semh[s], v))

    def bump(sem, n):
        sems[sem] += n
        return sems[sem]

    def dma(eng, sem, out, in_):
        _sem(sem)
        tok = bump(sem, 16)
        op_raw(eng, lambda e, o=out, i=in_, s=sem: e.dma_start(out=o, in_=i).then_inc(semh[s], 16))
        return tok

    def dmaT(eng, sem, out, in_):
        _sem(sem)
        tok = bump(sem, 16)
        op_raw(eng, lambda e, o=out, i=in_, s=sem: e.dma_start_transpose(out=o, in_=i).then_inc(semh[s], 16))
        return tok

    def inc(eng, sem):
        _sem(sem)
        tok = bump(sem, 1)
        if eng in auto_cnt:
            s_self = "AV" if eng == "vector" else "AA"
            k = auto_cnt[eng]
            op_raw(eng, lambda e, ss=s_self, k=k: e.wait_ge(semh[ss], k))
            op_raw(eng, lambda e, s=sem: e.sem_inc(semh[s], 1))
        else:
            prev = prog[eng].pop()

            def combined(e, p=prev, s=sem):
                p(e).then_inc(semh[s], 1)

            prog[eng].append(combined)
        return tok

    def sb(name, shape, dtype):
        return nc.alloc_sbuf_tensor(name, shape, dtype)

    ones_bf = sb("ones_bf", [1, 128], BF16)
    ones128 = sb("ones128", [128, 128], BF16)
    idn_f = sb("idn_f", [128, 128], F32)
    dmat_f = sb("dmat_f", [128, 128], F32)
    ltm_f = sb("ltm_f", [128, 128], F32)
    ip1_f = sb("ip1_f", [4, 128], F32)
    irev_f = sb("irev_f", [128, 1], F32)
    c128_f = sb("c128_f", [4, 1], F32)
    pexp_f = sb("pexp_f", [128, NPT], F32)
    pmask_f = sb("pmask_f", [128, NPT], F32)
    lamv = sb("lamv", [1, 16], F32)
    hilo = sb("hilo", [1, 16], BF16)
    llb = sb("llb", [128, 36], F32)
    tpA = sb("tpA", [32, 32], F32)
    tpB = sb("tpB", [32, 32], F32)
    tpC = sb("tpC", [32, 32], F32)
    tpD = sb("tpD", [32, 32], F32)
    lamC = sb("lamC", [4, 1], F32)
    gt_f = sb("gt_f", [128, 128], F32)
    gt_bf = sb("gt_bf", [128, 128], BF16)
    pt_bf = sb("pt_bf", [4, 128], BF16)
    qt_f = sb("qt_f", [128, 4], F32)
    qt_bf = sb("qt_bf", [128, 4], BF16)
    vt_f = sb("vt_f", [128, NPT * 4], F32)
    vt_bf = sb("vt_bf", [128, NPT * 4], BF16)
    ypad_f = sb("ypad_f", [32, D], F32)
    ypad_bf = sb("ypad_bf", [32, D], BF16)
    yt_bf = sb("yt_bf", [128, 16 * 32], BF16)
    c_f = sb("c_f", [4, D], F32)
    c_bf = sb("c_bf", [4, D], BF16)
    stat6 = sb("stat6", [128, 24], F32)
    stat2 = sb("stat2", [128, 2], F32)
    mu_n = sb("mu_n", [128, 1], F32)
    inv_s = sb("inv_s", [128, 1], F32)
    red1 = sb("red1", [128, 1], F32)
    g127 = sb("g127", [128, NCHUNK], F32)
    sg_a = sb("sg_a", [128, NCHUNK], F32)
    sv_a = sb("sv_a", [128, NCHUNK], F32)
    qs_a = sb("qs_a", [128, NCHUNK], F32)
    s2_a = sb("s2_a", [128, NCHUNK], F32)
    gmax = sb("gmax", [128, NCHUNK], F32)
    gwsum = sb("gwsum", [128, 3], F32)
    acc96 = sb("acc96", [128, 192], F32)
    gshlb = sb("gshlb", [128, 8], BF16)
    gwsc = sb("gwsc", [128, 8], F32)
    negsc = sb("negsc", [128, 4], F32)
    prow = sb("prow", [1, 16], BF16)
    tiny = sb("tiny", [4, 8], F32)
    rowf = sb("rowf", [1, 16], F32)
    gshf = sb("gshf", [128, 8], F32)

    POOL_BYTES = 180224
    POOL_BYTES = 179392
    pool = nc.alloc_sbuf_tensor("pool", [128, POOL_BYTES], mybir.dt.uint8)
    pool_base = nc.lookup_mloc(pool).addr

    class Ph:
        def __init__(self):
            self.off = 0

        def t(self, name, shape, dtype):
            nb = int(np.prod(shape[1:])) * mybir.dt.size(dtype)
            off = self.off
            self.off = (off + nb + 31) & ~31
            assert self.off <= POOL_BYTES, (name, self.off)
            return nc.alloc_sbuf_tensor_at(name, shape, dtype, offset=pool_base + off)

    A = Ph()
    wtb = A.t("wtb", [128, 16 * D], BF16)
    xT = A.t("xT", [128, 16 * TC], BF16)
    z2 = A.t("z2", [128, 2 * D], BF16)
    gam_b = A.t("gam_b", [128, D], F32)
    bet_b = A.t("bet_b", [128, D], F32)
    t_s1 = A.t("t_s1", [128, D], F32)
    u_s1 = A.t("u_s1", [128, D], F32)
    xf32 = A.t("xf32", [128, D], F32)
    xpc = A.t("xpc", [128, 2 * D], BF16)
    x1_c = A.t("x1_c", [128, D], F32)
    xqi = A.t("xqi", [128, D], I32)
    xqb = A.t("xqb", [128, D], BF16)

    B = Ph()
    xqT = B.t("xqT", [128, 16 * TC], BF16)
    gb_sl = B.t("gb_sl", [128, 2 * HB], F32)
    vb_sl = B.t("vb_sl", [128, 2 * HB], F32)
    wland = B.t("wland", [128, 2 * 2048], F32)
    gq_s = B.t("gq_s", [128, 2 * 16 * HB], BF16)
    vq_s = B.t("vq_s", [128, 2 * 16 * HB], BF16)
    wqi = B.t("wqi", [128, 2048], I32)
    gvst = B.t("gvst", [128, 8 * HB], BF16)
    gf_s = B.t("gf_s", [128, HB], F32)
    vf_s = B.t("vf_s", [128, HB], F32)
    gvc = B.t("gvc", [128, 2 * 4096], BF16)
    gvqi = B.t("gvqi", [128, 4096], I32)

    C = Ph()
    oacc = C.t("oacc", [128, NCHUNK * D], F32)
    owts = C.t("owts", [128, 2 * 8 * D], BF16)
    owl = C.t("owl", [128, D], F32)
    gvqT = C.t("gvqT", [128, 8 * TC], BF16)
    obb = C.t("obb", [128, D], F32)
    x1l = C.t("x1l", [128, D], F32)
    outst = x1l
    r2_sb = owl

    psA = nc.alloc_psum_tensor("psA", [128, 2048], F32)
    psB = nc.alloc_psum_tensor("psB", [128, 2048], F32)

    op("vector", lambda e: e.memset(llb[:, 8:36], 0.0))
    op("vector", lambda e: e.memset(tpC[:], 0.0))
    op("vector", lambda e: e.memset(tpD[:], 0.0))
    op("vector", lambda e: e.memset(prow[:], 0.0))
    inc("vector", "SV")

    def finish(nc_, prog_, semh_, sems_):
        dout_ap = dout.ap()
        _sem("GOUT")
        for n_ in list(sem_names):
            if sems_[n_] > 0:
                prog_["gpsimd"].append(lambda e, n_=n_, v=sems_[n_]: e.wait_ge(semh_[n_], v))
        n = sems_["GOUT"]
        for j in range(NCHUNK):
            n += 16
            prog_["gpsimd"].append(lambda e, j=j: e.dma_start(
                out=dout_ap[j * CH:(j + 1) * CH, :], in_=outst[:]).then_inc(semh_["GOUT"], 16))
        prog_["gpsimd"].append(lambda e, n=n: e.wait_ge(semh_["GOUT"], n))
        prog_["vector"].append(lambda e: e.memset(outst[:], 0.0))
        return _materialize(nc_, prog_, semh_, sem_names)

    def ln_stats(src):
        for i in range(4):
            op("vector", lambda e, i=i, s=src: e.bn_stats(out=stat6[:, i * 6:(i + 1) * 6],
                                                          in_=s[:, i * 512:(i + 1) * 512]))
        op("vector", lambda e: e.bn_aggr(out=stat2[:], in_=stat6[:]))
        op("vector", lambda e: e.tensor_scalar_mul(out=mu_n[:], in0=stat2[:, 0:1], scalar1=-1.0))
        op("vector", lambda e: e.tensor_scalar_add(out=red1[:], in0=stat2[:, 1:2], scalar1=LN_EPS))
        t = inc("vector", "SV")
        wait("scalar", "SV", t)
        op("scalar", lambda e: e.activation(out=inv_s[:], in_=red1[:], func=AF.Sqrt))
        t2 = inc("scalar", "SA")
        wait("vector", "SA", t2)
        op("vector", lambda e: e.reciprocal(out=inv_s[:], in_=inv_s[:]))

    t_consts = 0
    for (dst, src) in [(idn_f, didn), (dmat_f, ddmat), (ltm_f, dltm),
                       (ip1_f, dip1), (irev_f, direv), (c128_f, dc128),
                       (pexp_f, dpexp), (pmask_f, dpmask)]:
        t_consts = dma("sync", "YC", dst[:], src.ap())
    t_yc = dma("sync", "YC", lamv[:, 0:4], dlam.ap())
    t_yc = dma("sync", "YC", lamv[:, 4:8], deta.ap())
    t_gb_bc = dma("sync", "YGB", gam_b[:], dgam.ap().partition_broadcast(128))
    t_bb_bc = dma("sync", "YGB", bet_b[:], dbet.ap().partition_broadcast(128))
    t_ones = dma("gpsimd", "GON", ones_bf[:], dones.ap())
    t_wt = 0
    for dk in range(16):
        t_wt = dma("gpsimd", "GW", wtb[:, dk * D:(dk + 1) * D],
                   dwt.ap()[dk * 128:(dk + 1) * 128, :])

    xb_st = [0] * NCHUNK
    for j in range(NCHUNK):
        sl = xpc[:, (j % 2) * D:(j % 2) * D + D]
        if j >= 2:
            wait("gpsimd", "GXS", xb_st[j - 2])
        t_l = dma("gpsimd", "GX", sl, dx.ap()[j * CH:(j + 1) * CH, :])
        wait("gpsimd", "GX", t_l)
        xb_st[j] = dma("gpsimd", "GXS", sxbf.ap()[j * CH:(j + 1) * CH, :], sl)
    t_xbf_w = xb_st[-1]
    wait("sync", "GXS", t_xbf_w)
    t_xT = 0
    for dk in range(16):
        t_xT = dmaT("sync", "YXT", xT[:, dk * TC:(dk + 1) * TC],
                    sxbf.ap()[:, dk * 128:(dk + 1) * 128])

    if limit < 1:
        return finish(nc, prog, semh, sems)
    wait("scalar", "YC", t_yc)
    op("scalar", lambda e: e.activation(out=lamv[:, 8:12], in_=lamv[:, 0:4], func=AF.Sigmoid))
    t = inc("scalar", "SA")
    wait("vector", "SA", t)
    op("vector", lambda e: e.tensor_scalar_max(out=lamv[:, 8:12], in0=lamv[:, 8:12], scalar1=1e-38))
    t = inc("vector", "SV")
    wait("scalar", "SV", t)
    op("scalar", lambda e: e.activation(out=lamv[:, 12:16], in_=lamv[:, 8:12], func=AF.Ln))
    t = inc("scalar", "SA")
    wait("vector", "SA", t)
    op("vector", lambda e: e.tensor_copy(out=hilo[:, 0:4], in_=lamv[:, 12:16]))
    op("vector", lambda e: e.tensor_copy(out=hilo[:, 4:8], in_=lamv[:, 4:8]))
    op("vector", lambda e: e.tensor_copy(out=rowf[0:1, 0:4], in_=hilo[:, 0:4]))
    op("vector", lambda e: e.tensor_copy(out=rowf[0:1, 4:8], in_=hilo[:, 4:8]))
    op("vector", lambda e: e.tensor_sub(out=rowf[0:1, 8:12], in0=lamv[:, 12:16], in1=rowf[0:1, 0:4]))
    op("vector", lambda e: e.tensor_sub(out=rowf[0:1, 12:16], in0=lamv[:, 4:8], in1=rowf[0:1, 4:8]))
    op("vector", lambda e: e.tensor_copy(out=hilo[:, 8:16], in_=rowf[0:1, 8:16]))
    t_hilo = inc("vector", "SV")
    wait("tensor", "GON", t_ones)
    wait("tensor", "SV", t_hilo)
    op("tensor", lambda e: e.matmul(psB[:, 0:8], ones_bf[:], hilo[:, 0:8], start=True, stop=False))
    op("tensor", lambda e: e.matmul(psB[:, 0:8], ones_bf[:], hilo[:, 8:16], start=False, stop=True))
    op("tensor", lambda e: e.matmul(psA[:, 0:128], ones_bf[:], ones_bf[:], start=True, stop=True))
    t_bc1 = inc("tensor", "ST")
    wait("vector", "ST", t_bc1)
    op("vector", lambda e: e.tensor_copy(out=llb[:, 0:8], in_=psB[:, 0:8]))
    op("vector", lambda e: e.tensor_copy(out=ones128[:], in_=psA[:, 0:128]))
    op("vector", lambda e: e.transpose(out=tpA[:], in_=llb[0:32, 0:32]))
    op("vector", lambda e: e.transpose(out=tpB[:], in_=llb[0:32, 4:36]))
    t_llp = inc("vector", "SV")
    ll_p = tpA
    eta_p = tpB

    wait("scalar", "YC", t_yc)
    wait("scalar", "SV", t_llp)
    op("vector", lambda e: e.memset(gt_f[:], 0.0))
    t_g0 = inc("vector", "SV")
    for k in range(4):
        op("scalar", lambda e, k=k: e.activation(out=u_s1[:, 0:128], in_=dmat_f[:], func=AF.Exp,
                                                 scale=llb[:, k:k + 1]))
        t_e = inc("scalar", "SA")
        wait("vector", "SA", t_e)
        if k == 0:
            wait("vector", "SV", t_g0)
        op("vector", lambda e, k=k: e.scalar_tensor_tensor(out=gt_f[:], in0=u_s1[:, 0:128],
                                                           scalar=llb[:, 4 + k:5 + k], in1=gt_f[:],
                                                           op0=ALU.mult, op1=ALU.add))
        t_gk = inc("vector", "SV")
        wait("scalar", "SV", t_gk)
    op("vector", lambda e: e.tensor_mul(out=gt_f[:], in0=gt_f[:], in1=ltm_f[:]))
    op("vector", lambda e: e.tensor_add(out=gt_f[:], in0=gt_f[:], in1=idn_f[:]))
    op("vector", lambda e: e.tensor_copy(out=gt_bf[:], in_=gt_f[:]))
    t_gt = inc("vector", "SV")

    op("scalar", lambda e: e.activation(out=ypad_f[0:4, 0:128], in_=ip1_f[:], func=AF.Exp,
                                        scale=ll_p[0:4, 0:1]))
    t = inc("scalar", "SA")
    wait("vector", "SA", t)
    op("vector", lambda e: e.tensor_scalar_mul(out=ypad_f[0:4, 0:128], in0=ypad_f[0:4, 0:128],
                                               scalar1=eta_p[0:4, 0:1]))
    op("vector", lambda e: e.tensor_copy(out=pt_bf[:], in_=ypad_f[0:4, 0:128]))
    t_ptb = inc("vector", "SV")
    for k in range(4):
        op("scalar", lambda e, k=k: e.activation(out=qt_f[:, k:k + 1], in_=irev_f[:], func=AF.Exp,
                                                 scale=llb[:, k:k + 1]))
    op("scalar", lambda e: e.activation(out=lamC[:], in_=c128_f[:], func=AF.Exp, scale=ll_p[0:4, 0:1]))
    for k in range(4):
        op("scalar", lambda e, k=k: e.activation(out=vt_f[:, k:NPT * 4:4], in_=pexp_f[:], func=AF.Exp,
                                                 scale=llb[:, k:k + 1]))
    t_acts = inc("scalar", "SA")
    wait("vector", "SA", t_acts)
    op("vector", lambda e: e.tensor_copy(out=qt_bf[:], in_=qt_f[:]))
    for k in range(4):
        op("vector", lambda e, k=k: e.tensor_mul(out=vt_f[:, k:NPT * 4:4], in0=vt_f[:, k:NPT * 4:4],
                                                 in1=pmask_f[:]))
    op("vector", lambda e: e.tensor_copy(out=vt_bf[:], in_=vt_f[:]))
    t_vtb = inc("vector", "SV")

    if limit < 2:
        return finish(nc, prog, semh, sems)
    wait("tensor", "SV", t_vtb)
    mm_y_toks = [0] * NPT
    for st in range(NPT):
        sl = xpc[:, (st % 2) * D:(st % 2) * D + D]
        if st < 2:
            wait("gpsimd", "GXS", t_xbf_w)
        else:
            wait("gpsimd", "ST", mm_y_toks[st - 2])
        t_l = dma("gpsimd", "GP%d" % (st % 2), sl, dxprev.ap()[st * CH:(st + 1) * CH, :])
        wait("tensor", "GP%d" % (st % 2), t_l)
        for q in range(4):
            cs = slice(q * 512, q * 512 + 512)
            op("tensor", lambda e, st=st, sl=sl, cs=cs:
               e.matmul(psA[0:4, cs], vt_bf[:, st * 4:(st + 1) * 4], sl[:, cs],
                        start=(st == 0), stop=(st == NPT - 1)))
        mm_y_toks[st] = inc("tensor", "ST")
    mm_y = mm_y_toks[-1]
    if limit < 1.3:
        return finish(nc, prog, semh, sems)
    op("vector", lambda e: e.memset(ypad_f[:], 0.0))
    t_ymz = inc("vector", "SV")
    wait("scalar", "ST", mm_y)
    wait("scalar", "SV", t_ymz)
    op("scalar", lambda e: e.copy(out=ypad_f[0:4, :], in_=psA[0:4, :]))
    t = inc("scalar", "SA")
    wait("vector", "SA", t)
    op("vector", lambda e: e.tensor_copy(out=ypad_bf[:], in_=ypad_f[:]))
    for dk in range(16):
        for b in range(4):
            op("vector", lambda e, dk=dk, b=b:
               e.transpose(out=yt_bf[32 * b:32 * b + 32, dk * 32:dk * 32 + 32],
                           in_=ypad_bf[0:32, dk * 128 + 32 * b:dk * 128 + 32 * b + 32]))
    t_yt = inc("vector", "SV")
    if limit < 1.7:
        return finish(nc, prog, semh, sems)
    wait("tensor", "SV", t_yt)
    wait("tensor", "GW", t_wt)
    for dk in range(16):
        for q in range(4):
            cs = slice(q * 512, q * 512 + 512)
            op("tensor", lambda e, dk=dk, cs=cs:
               e.matmul(psB[0:4, cs], yt_bf[:, dk * 32:dk * 32 + 4],
                        wtb[:, dk * D + cs.start:dk * D + cs.stop],
                        start=(dk == 0), stop=(dk == 15)))
    mm_c = inc("tensor", "ST")
    if limit < 1.9:
        return finish(nc, prog, semh, sems)
    wait("scalar", "ST", mm_c)
    op("scalar", lambda e: e.copy(out=c_f[:], in_=psB[0:4, :]))
    t = inc("scalar", "SA")
    wait("vector", "SA", t)
    op("vector", lambda e: e.tensor_copy(out=c_bf[:], in_=c_f[:]))
    t_cbf = inc("vector", "SV")
    if limit < 1.95:
        return finish(nc, prog, semh, sems)

    if limit < 2:
        return finish(nc, prog, semh, sems)
    wait("tensor", "YXT", t_xT)
    wait("tensor", "SA", sems["SA"])
    t_sev = 0
    t_hfree = 0
    t_zuse = [0, 0]
    t_x1w = 0
    t_xqw = 0
    t_xf_use = 0
    mm_h_tok = [0] * NCHUNK
    first_ep = True
    _nch = 1 if limit < 2.08 else (2 if limit < 2.1 else NCHUNK)
    for j in range(_nch):
        slz = z2[:, (j % 2) * D:(j % 2) * D + D]
        if j >= 1:
            wait("sync", "SV", t_xf_use)
        t_xf = dma("sync", "YXF", xf32[:], dx.ap()[j * CH:(j + 1) * CH, :])

        wait("tensor", "SV", t_hfree)
        for dk in range(16):
            for q in range(4):
                cs = slice(q * 512, q * 512 + 512)
                op("tensor", lambda e, j=j, dk=dk, cs=cs:
                   e.matmul(psA[:, cs], xT[:, dk * TC + j * CH: dk * TC + (j + 1) * CH],
                            wtb[:, dk * D + cs.start:dk * D + cs.stop],
                            start=(dk == 0), stop=(dk == 15)))
        mm_z = inc("tensor", "ST")
        wait("scalar", "ST", mm_z)
        if j >= 2:
            wait("scalar", "ST", mm_h_tok[j - 2])
        op("scalar", lambda e, slz=slz: e.copy(out=slz, in_=psA[:]))
        t_zev = inc("scalar", "SA")
        if limit < 2.3:
            t_hfree = 0
            wait("tensor", "SA", t_zev)
            continue

        wait("tensor", "SV", t_sev)
        wait("tensor", "SA", t_zev)
        if j == 0:
            wait("tensor", "SV", t_cbf)
            wait("tensor", "SV", t_gt)
            wait("tensor", "SV", t_ptb)
        for q in range(4):
            cs = slice(q * 512, q * 512 + 512)
            op("tensor", lambda e, slz=slz, cs=cs:
               e.matmul(psB[0:4, cs], qt_bf[:], slz[:, cs], start=True, stop=True))
        mm_s = inc("tensor", "ST")
        for q in range(4):
            cs = slice(q * 512, q * 512 + 512)
            op("tensor", lambda e, slz=slz, cs=cs:
               e.matmul(psA[:, cs], gt_bf[:], slz[:, cs], start=True, stop=False))
        for q in range(4):
            cs = slice(q * 512, q * 512 + 512)
            op("tensor", lambda e, cs=cs:
               e.matmul(psA[:, cs], pt_bf[:], c_bf[:, cs], start=False, stop=True))
        mm_h = inc("tensor", "ST")
        mm_h_tok[j] = mm_h
        wait("vector", "ST", mm_s)
        op("vector", lambda e: e.scalar_tensor_tensor(out=c_f[:], in0=c_f[:], scalar=lamC[:, 0:1],
                                                      in1=psB[0:4, :], op0=ALU.mult, op1=ALU.add))
        op("vector", lambda e: e.tensor_copy(out=c_bf[:], in_=c_f[:]))
        t_sev = inc("vector", "SV")
        if limit < 2.5:
            wait("vector", "ST", mm_h)
            op("vector", lambda e: e.tensor_copy(out=t_s1[:], in_=psA[:]))
            t_hfree = inc("vector", "SV")
            continue
        wait("vector", "ST", mm_h)
        if first_ep:
            wait("vector", "YGB", t_bb_bc)
            first_ep = False
        ln_stats(psA)
        op("vector", lambda e: e.tensor_scalar(out=t_s1[:], in0=psA[:], scalar1=mu_n[:, 0:1],
                                               scalar2=inv_s[:, 0:1], op0=ALU.add, op1=ALU.mult))
        t_hfree = inc("vector", "SV")
        op("vector", lambda e: e.tensor_mul(out=u_s1[:], in0=t_s1[:], in1=gam_b[:]))
        wait("vector", "YXF", t_xf)
        op("vector", lambda e: e.tensor_add(out=t_s1[:], in0=xf32[:], in1=bet_b[:]))
        t_xf_use = inc("vector", "SV")
        op("vector", lambda e: e.tensor_add(out=u_s1[:], in0=u_s1[:], in1=t_s1[:]))
        ln_stats(u_s1)
        op("vector", lambda e: e.tensor_scalar(out=x1_c[:], in0=u_s1[:], scalar1=mu_n[:, 0:1],
                                               scalar2=inv_s[:, 0:1], op0=ALU.add, op1=ALU.mult))
        op("vector", lambda e: e.tensor_reduce(out=red1[:], in_=x1_c[:], axis=AX.X,
                                               op=ALU.max, apply_absolute_value=True))
        op("vector", lambda e: e.tensor_scalar_max(out=red1[:], in0=red1[:], scalar1=1e-5))
        op("vector", lambda e, j=j: e.tensor_scalar_mul(out=g127[:, j:j + 1], in0=red1[:],
                                                        scalar1=1.0 / 127.0))
        op("vector", lambda e: e.reciprocal(out=red1[:], in_=red1[:]))
        op("vector", lambda e: e.tensor_scalar_mul(out=red1[:], in0=red1[:], scalar1=127.0))
        if j >= 1:
            wait("vector", "GX1", t_x1w)
            wait("vector", "GXQ", t_xqw)
        op("vector", lambda e: e.tensor_scalar(out=u_s1[:], in0=x1_c[:], scalar1=red1[:, 0:1],
                                               scalar2=RB + 0.5, op0=ALU.mult, op1=ALU.add))
        op("vector", lambda e: e.tensor_copy(out=xqi[:], in_=u_s1[:]))
        op("vector", lambda e: e.tensor_tensor(out=t_s1[:], in0=xqi[:], in1=u_s1[:],
                                               op=ALU.is_gt))
        op("vector", lambda e: e.scalar_tensor_tensor(out=xqb[:], in0=xqi[:], scalar=-RB,
                                                      in1=t_s1[:], op0=ALU.add,
                                                      op1=ALU.subtract))
        t_q = inc("vector", "SV")
        wait("gpsimd", "SV", t_q)
        t_x1w = dma("gpsimd", "GX1", sx1.ap()[j * CH:(j + 1) * CH, :], x1_c[:])
        t_xqw = dma("gpsimd", "GXQ", sxq.ap()[j * CH:(j + 1) * CH, :], xqb[:])

    if limit < 3:
        return finish(nc, prog, semh, sems)
    tile_list = []
    for (src, ntile, freew, m) in [(dgwt, 16, H, 0), (dvwt, 16, H, 1), (dowt, 64, D, 2)]:
        nsub = freew // 2048
        for i in range(ntile):
            for s in range(nsub):
                tile_list.append((src, i, s, m))
    t_gred = [0] * len(tile_list)
    for idx, (src, i, s, m) in enumerate(tile_list):
        slot = idx % 2
        sl = wland[:, slot * 2048:(slot + 1) * 2048]
        if idx >= 2:
            wait("scalar", "AA", t_gred[idx - 2])
        tok = dma("scalar", "C%d" % (idx % 2), sl,
                  src.ap()[i * 128:(i + 1) * 128, s * 2048:(s + 1) * 2048])
        wait("scalar", "C%d" % (idx % 2), tok)
        op("scalar", lambda e, sl=sl, idx=idx: e.activation(
            out=sl, in_=sl, func=AF.Abs, accum_out=acc96[:, idx:idx + 1]))
        t_gred[idx] = auto_cnt["scalar"]
    t_gacc = inc("scalar", "SA")
    wait("vector", "SA", t_gacc)
    op("vector", lambda e: e.tensor_reduce(out=gwsum[:, 0:1], in_=acc96[:, 0:64], axis=AX.X,
                                           op=ALU.add))
    op("vector", lambda e: e.tensor_reduce(out=gwsum[:, 1:2], in_=acc96[:, 64:128], axis=AX.X,
                                           op=ALU.add))
    op("vector", lambda e: e.tensor_reduce(out=gwsum[:, 2:3], in_=acc96[:, 128:192], axis=AX.X,
                                           op=ALU.add))
    op("vector", lambda e: e.tensor_copy(out=gshlb[:, 0:3], in_=gwsum[:]))
    op("vector", lambda e: e.tensor_copy(out=gshf[:, 0:3], in_=gshlb[:, 0:3]))
    op("vector", lambda e: e.tensor_sub(out=gshf[:, 4:7], in0=gwsum[:], in1=gshf[:, 0:3]))
    op("vector", lambda e: e.tensor_copy(out=gshlb[:, 4:7], in_=gshf[:, 4:7]))
    t_hl = inc("vector", "SV")
    wait("tensor", "SV", t_hl)
    op("tensor", lambda e: e.matmul(psB[0:3, 0:128], gshlb[:, 0:3], ones128[:], start=True, stop=False))
    op("tensor", lambda e: e.matmul(psB[0:3, 0:128], gshlb[:, 4:7], ones128[:], start=False, stop=True))
    t_tot = inc("tensor", "ST")
    wait("vector", "ST", t_tot)
    op("vector", lambda e: e.tensor_copy(out=tiny[0:3, 0:1], in_=psB[0:3, 0:1]))
    op("vector", lambda e: e.tensor_scalar_mul(out=tiny[0:3, 1:2], in0=tiny[0:3, 0:1],
                                               scalar1=1.0 / float(D * H)))
    op("vector", lambda e: e.tensor_scalar_max(out=tiny[0:3, 1:2], in0=tiny[0:3, 1:2], scalar1=1e-5))
    op("vector", lambda e: e.reciprocal(out=tiny[0:3, 2:3], in_=tiny[0:3, 1:2]))
    op("vector", lambda e: e.tensor_copy(out=tpC[0:3, 0:1], in_=tiny[0:3, 1:2]))
    op("vector", lambda e: e.tensor_copy(out=tpD[0:3, 0:1], in_=tiny[0:3, 2:3]))
    op("vector", lambda e: e.transpose(out=tpA[:], in_=tpC[:]))
    op("vector", lambda e: e.transpose(out=tpB[:], in_=tpD[:]))
    op("vector", lambda e: e.tensor_copy(out=rowf[0:1, 0:3], in_=tpA[0:1, 0:3]))
    op("vector", lambda e: e.tensor_copy(out=rowf[0:1, 3:6], in_=tpB[0:1, 0:3]))
    op("vector", lambda e: e.tensor_copy(out=prow[:, 0:6], in_=rowf[0:1, 0:6]))
    op("vector", lambda e: e.tensor_copy(out=rowf[0:1, 8:14], in_=prow[:, 0:6]))
    op("vector", lambda e: e.tensor_sub(out=lamv[:, 0:6], in0=rowf[0:1, 0:6], in1=rowf[0:1, 8:14]))
    op("vector", lambda e: e.tensor_copy(out=prow[:, 8:14], in_=lamv[:, 0:6]))
    t_ghl = inc("vector", "SV")
    wait("tensor", "SV", t_ghl)
    op("tensor", lambda e: e.matmul(psA[:, 0:8], ones_bf[:], prow[:, 0:8], start=True, stop=False))
    op("tensor", lambda e: e.matmul(psA[:, 0:8], ones_bf[:], prow[:, 8:16], start=False, stop=True))
    t_gw2 = inc("tensor", "ST")
    wait("vector", "ST", t_gw2)
    op("vector", lambda e: e.tensor_copy(out=gwsc[:], in_=psA[:, 0:8]))
    for m in range(3):
        op("vector", lambda e, m=m: e.tensor_scalar_mul(out=negsc[:, m:m + 1],
                                                        in0=gwsc[:, 3 + m:4 + m], scalar1=-1.0))
    for j in range(NCHUNK):
        op("vector", lambda e, j=j: e.tensor_mul(out=sg_a[:, j:j + 1], in0=g127[:, j:j + 1],
                                                 in1=gwsc[:, 0:1]))
        op("vector", lambda e, j=j: e.tensor_mul(out=sv_a[:, j:j + 1], in0=g127[:, j:j + 1],
                                                 in1=gwsc[:, 1:2]))
    t_scales = inc("vector", "SV")

    wait("sync", "GXQ", t_xqw)
    t_xqT = 0
    for dk in range(16):
        t_xqT = dmaT("sync", "YXQ", xqT[:, dk * TC:(dk + 1) * TC],
                     sxq.ap()[:, dk * 128:(dk + 1) * 128])

    if limit < 4:
        return finish(nc, prog, semh, sems)
    wq_slot_tok = [0, 0]
    wq_wqi_tok = [0]

    def emit_wq(hb, buf, src, invcol, ninvcol, t_prev_use):
        t_rdy2 = 0
        for span in range(4):
            slot = span % 2
            sl = wland[:, slot * 2048:(slot + 1) * 2048]
            wait("scalar", "SV", wq_slot_tok[slot])
            tok = dma("scalar", "C%d" % (span % 2), sl,
                      src.ap().rearrange("(b p) h -> p b h", p=128)
                      [:, span * 4:(span + 1) * 4, hb * HB:(hb + 1) * HB])
            wait("vector", "C%d" % (span % 2), tok)
            if t_prev_use:
                wait("vector", "ST", t_prev_use)
                t_prev_use = 0
            op("vector", lambda e, sl=sl, inv=invcol: e.tensor_scalar(
                out=wqi[:].bitcast(F32), in0=sl, scalar1=inv, scalar2=0.5,
                op0=ALU.mult, op1=ALU.is_ge))
            op("vector", lambda e, sl=sl, ninv=ninvcol: e.tensor_scalar(
                out=sl, in0=sl, scalar1=ninv, scalar2=0.5, op0=ALU.mult, op1=ALU.is_ge))
            op("vector", lambda e, sl=sl, buf=buf, span=span: e.tensor_sub(
                out=buf[:, span * 2048:(span + 1) * 2048], in0=wqi[:].bitcast(F32), in1=sl))
            t_rdy2 = inc("vector", "SV")
            wq_slot_tok[slot] = t_rdy2
        return t_rdy2

    t_guse = [0, 0]
    t_vuse = [0, 0]
    t_gv_w = 0
    t_evac_v = {0: 0, 1: 0}
    t_evac_a = {0: 0, 1: 0}
    t_gvr = 0
    t_bsl_use = [0, 0]
    for hb in range(NHB):
        pbuf = hb % 2
        gbuf = gq_s[:, pbuf * 16 * HB:(pbuf + 1) * 16 * HB]
        vbuf = vq_s[:, pbuf * 16 * HB:(pbuf + 1) * 16 * HB]
        wait("sync", "SV", t_bsl_use[pbuf])
        t_gbs = dma("sync", "YB%d" % pbuf, gb_sl[:, pbuf * HB:(pbuf + 1) * HB],
                    dgb.ap()[0:1, hb * HB:(hb + 1) * HB].partition_broadcast(128))
        t_vbs = dma("sync", "YB%d" % pbuf, vb_sl[:, pbuf * HB:(pbuf + 1) * HB],
                    dvb.ap()[0:1, hb * HB:(hb + 1) * HB].partition_broadcast(128))
        if hb == 0:
            wait("vector", "SV", t_scales)
        t_gq = emit_wq(hb, gbuf, dgwt, gwsc[:, 3:4], negsc[:, 0:1], t_guse[pbuf])
        t_vq = emit_wq(hb, vbuf, dvwt, gwsc[:, 4:5], negsc[:, 1:2], t_vuse[pbuf])
        if hb == 0:
            wait("tensor", "YXQ", t_xqT)
        wait("tensor", "SV", t_gq)
        wait("tensor", "SV", t_vq)
        mm_g = mm_v = 0
        t_ev_last = 0
        for tt in range(8):
            pg = psA[:, (tt % 2) * 512:(tt % 2) * 512 + 512]
            pv = psB[:, (tt % 2) * 512:(tt % 2) * 512 + 512]
            wait("tensor", "SV", t_evac_v[tt % 2])
            wait("tensor", "SA", t_evac_a[tt % 2])
            for dk in range(16):
                op("tensor", lambda e, dk=dk, tt=tt, pg=pg, g=gbuf:
                   e.matmul(pg, xqT[:, dk * TC + tt * CH: dk * TC + (tt + 1) * CH],
                            g[:, dk * HB:(dk + 1) * HB], start=(dk == 0), stop=(dk == 15)))
            mm_g = inc("tensor", "ST")
            for dk in range(16):
                op("tensor", lambda e, dk=dk, tt=tt, pv=pv, v=vbuf:
                   e.matmul(pv, xqT[:, dk * TC + tt * CH: dk * TC + (tt + 1) * CH],
                            v[:, dk * HB:(dk + 1) * HB], start=(dk == 0), stop=(dk == 15)))
            mm_v = inc("tensor", "ST")
            wait("vector", "ST", mm_g)
            if tt == 0:
                wait("vector", "YB%d" % pbuf, t_vbs)
            op("vector", lambda e, pg=pg, tt=tt, pbuf=pbuf: e.scalar_tensor_tensor(
                out=gf_s[:], in0=pg, scalar=sg_a[:, tt:tt + 1],
                in1=gb_sl[:, pbuf * HB:(pbuf + 1) * HB], op0=ALU.mult, op1=ALU.add))
            t_ge = inc("vector", "SV")
            wait("scalar", "SV", t_ge)
            op("scalar", lambda e: e.activation(out=gf_s[:], in_=gf_s[:], func=AF.Silu))
            t_si = inc("scalar", "SA")
            wait("vector", "ST", mm_v)
            op("vector", lambda e, pv=pv, tt=tt, pbuf=pbuf: e.scalar_tensor_tensor(
                out=vf_s[:], in0=pv, scalar=sv_a[:, tt:tt + 1],
                in1=vb_sl[:, pbuf * HB:(pbuf + 1) * HB], op0=ALU.mult, op1=ALU.add))
            t_evac_v[tt % 2] = inc("vector", "SV")
            t_evac_a[tt % 2] = t_si
            wait("vector", "SA", t_si)
            if hb >= 1 and tt == 0:
                wait("vector", "GGV", t_gv_w)
            op("vector", lambda e, tt=tt: e.tensor_mul(out=gvst[:, tt * HB:(tt + 1) * HB],
                                                       in0=gf_s[:], in1=vf_s[:]))
            op("vector", lambda e, tt=tt: e.tensor_reduce(out=red1[:],
                                                          in_=gvst[:, tt * HB:(tt + 1) * HB],
                                                          axis=AX.X, op=ALU.max,
                                                          apply_absolute_value=True))
            if hb == 0:
                op("vector", lambda e, tt=tt: e.tensor_copy(out=gmax[:, tt:tt + 1], in_=red1[:]))
            else:
                op("vector", lambda e, tt=tt: e.tensor_max(out=gmax[:, tt:tt + 1],
                                                           in0=gmax[:, tt:tt + 1], in1=red1[:]))
            t_gvr = inc("vector", "SV")
            t_ev_last = t_gvr
        t_bsl_use[pbuf] = t_ev_last
        wait("gpsimd", "SV", t_gvr)
        t_gv_w = dma("gpsimd", "GGV",
                     sgv.ap().rearrange("(a p) h -> p a h", p=128)[:, :, hb * HB:(hb + 1) * HB],
                     gvst[:].rearrange("p (a h) -> p a h", h=HB))
        t_guse[pbuf] = mm_g
        t_vuse[pbuf] = mm_v

    if limit < 5:
        return finish(nc, prog, semh, sems)
    for j in range(NCHUNK):
        op("vector", lambda e, j=j: e.tensor_scalar_max(out=gmax[:, j:j + 1],
                                                        in0=gmax[:, j:j + 1], scalar1=1e-5))
        op("vector", lambda e, j=j: e.reciprocal(out=qs_a[:, j:j + 1], in_=gmax[:, j:j + 1]))
        op("vector", lambda e, j=j: e.tensor_scalar_mul(out=qs_a[:, j:j + 1], in0=qs_a[:, j:j + 1],
                                                        scalar1=127.0))
        op("vector", lambda e, j=j: e.tensor_mul(out=s2_a[:, j:j + 1], in0=gmax[:, j:j + 1],
                                                 in1=gwsc[:, 2:3]))
        op("vector", lambda e, j=j: e.tensor_scalar_mul(out=s2_a[:, j:j + 1], in0=s2_a[:, j:j + 1],
                                                        scalar1=1.0 / 127.0))
    t_qs = inc("vector", "SV")
    wait("sync", "GGV", t_gv_w)
    t_gvq_st = [0] * (2 * NCHUNK)
    for jh in range(2 * NCHUNK):
        j, hhalf = jh // 2, jh % 2
        sl = gvc[:, (jh % 2) * 4096:(jh % 2) * 4096 + 4096]
        if jh >= 2:
            wait("sync", "GGQ", sems["GGQ"])
        tok = dma("sync", "YV%d" % (jh % 2), sl,
                  sgv.ap()[j * CH:(j + 1) * CH, hhalf * 4096:(hhalf + 1) * 4096])
        wait("vector", "YV%d" % (jh % 2), tok)
        if jh == 0:
            wait("vector", "SV", t_qs)
        tfv = gq_s[:].bitcast(F32)
        op("vector", lambda e, sl=sl, j=j, tfv=tfv: e.tensor_scalar(
            out=tfv[:, 0:4096], in0=sl, scalar1=qs_a[:, j:j + 1], scalar2=RB + 0.5,
            op0=ALU.mult, op1=ALU.add))
        op("vector", lambda e, tfv=tfv: e.tensor_copy(out=gvqi[:], in_=tfv[:, 0:4096]))
        op("vector", lambda e, tfv=tfv: e.tensor_tensor(out=tfv[:, 4096:8192], in0=gvqi[:],
                                                        in1=tfv[:, 0:4096], op=ALU.is_gt))
        op("vector", lambda e, sl=sl, tfv=tfv: e.scalar_tensor_tensor(
            out=sl, in0=gvqi[:], scalar=-RB, in1=tfv[:, 4096:8192],
            op0=ALU.add, op1=ALU.subtract))
        t_gq2 = inc("vector", "SV")
        wait("gpsimd", "SV", t_gq2)
        t_gvq_st[jh] = dma("gpsimd", "GGQ",
                           sgvq.ap()[j * CH:(j + 1) * CH, hhalf * 4096:(hhalf + 1) * 4096], sl)
    t_gvq_w = t_gvq_st[-1]

    if limit < 6:
        return finish(nc, prog, semh, sems)
    wait("sync", "GGQ", t_gvq_w)
    t_ob_bc = dma("sync", "YOB", obb[:], dob.ap().partition_broadcast(128))

    t_oacc = {0: 0, 1: 0}
    t_oacc_fin = [0] * 8
    t_owu = [0, 0]
    t_gvTu = 0
    mm_o = 0
    ow_q_tok = [0]
    _nsh = 1 if limit < 6.35 else 8
    for sh in range(_nsh):
        sbuf_i = sh % 2
        owsl = owts[:, sbuf_i * 8 * D:(sbuf_i + 1) * 8 * D]
        if sh >= 2:
            wait("gpsimd", "ST", t_owu[sbuf_i])
        t_owl = 0
        if sh == 0:
            wait("gpsimd", "GGV", t_gv_w)
        for sub in range(8):
            wait("gpsimd", "SV", ow_q_tok[0])
            t_l = dma("gpsimd", "GOW", owl[:],
                      dowt.ap()[sh * 1024 + sub * 128: sh * 1024 + (sub + 1) * 128, :])
            wait("vector", "GOW", t_l)
            if sh >= 2 and sub == 0:
                wait("vector", "ST", t_owu[sbuf_i])
            op("vector", lambda e: e.tensor_scalar(
                out=x1l[:], in0=owl[:], scalar1=gwsc[:, 5:6], scalar2=0.5,
                op0=ALU.mult, op1=ALU.is_ge))
            op("vector", lambda e: e.tensor_scalar(
                out=owl[:], in0=owl[:], scalar1=negsc[:, 2:3], scalar2=0.5,
                op0=ALU.mult, op1=ALU.is_ge))
            op("vector", lambda e, owsl=owsl, sub=sub: e.tensor_sub(
                out=owsl[:, sub * D:(sub + 1) * D], in0=x1l[:], in1=owl[:]))
            t_owl = inc("vector", "SV")
            ow_q_tok[0] = t_owl
        if sh == 0:
            wait("sync", "GGQ", t_gvq_w)
        else:
            wait("sync", "ST", t_gvTu)
        t_gvl = 0
        for sub in range(8):
            hk = sh * 8 + sub
            t_gvl = dmaT("sync", "YGT", gvqT[:, sub * TC:(sub + 1) * TC],
                         sgvq.ap()[:, hk * 128:(hk + 1) * 128])
        wait("tensor", "SV", t_owl)
        wait("tensor", "YGT", t_gvl)
        for tt in range(8):
            ps = psA if tt % 2 == 0 else psB
            wait("tensor", "SV", t_oacc[tt % 2])
            for sub in range(8):
                for q in range(4):
                    cs = slice(q * 512, q * 512 + 512)
                    op("tensor", lambda e, sub=sub, tt=tt, cs=cs, ps=ps, owsl=owsl:
                       e.matmul(ps[:, cs],
                                gvqT[:, sub * TC + tt * CH: sub * TC + (tt + 1) * CH],
                                owsl[:, sub * D + cs.start:sub * D + cs.stop],
                                start=(sub == 0), stop=(sub == 7)))
            mm_o = inc("tensor", "ST")
            wait("vector", "ST", mm_o)
            if sh == 0:
                op("vector", lambda e, tt=tt, ps=ps: e.tensor_copy(
                    out=oacc[:, tt * D:(tt + 1) * D], in_=ps[:]))
            else:
                op("vector", lambda e, tt=tt, ps=ps: e.tensor_add(
                    out=oacc[:, tt * D:(tt + 1) * D], in0=oacc[:, tt * D:(tt + 1) * D], in1=ps[:]))
            t_oacc[tt % 2] = inc("vector", "SV")
            if sh == 7:
                t_oacc_fin[tt] = t_oacc[tt % 2]
        t_owu[sbuf_i] = mm_o
        t_gvTu = mm_o

    if limit < 6.5:
        return finish(nc, prog, semh, sems)
    if limit < 7:
        return finish(nc, prog, semh, sems)
    t_out_st = 0
    t_x1_use = 0
    t_out_use = 0
    first6 = True
    for j in range(NCHUNK):
        if j >= 1:
            wait("sync", "SV", t_x1_use)
            wait("sync", "GOUT", t_out_st)
        else:
            wait("sync", "SV", ow_q_tok[0])
        t_x1r = dma("sync", "YX1", x1l[:], sx1.ap()[j * CH:(j + 1) * CH, :])
        wait("vector", "SV", t_oacc_fin[j])
        wait("vector", "YX1", t_x1r)
        if first6:
            wait("vector", "YOB", t_ob_bc)
            first6 = False
        op("vector", lambda e: e.tensor_add(out=r2_sb[:], in0=x1l[:], in1=obb[:]))
        op("vector", lambda e, j=j: e.scalar_tensor_tensor(out=r2_sb[:],
                                                           in0=oacc[:, j * D:(j + 1) * D],
                                                           scalar=s2_a[:, j:j + 1], in1=r2_sb[:],
                                                           op0=ALU.mult, op1=ALU.add))
        ln_stats(r2_sb)
        if j >= 1:
            wait("vector", "GOUT", t_out_st)
        op("vector", lambda e: e.tensor_scalar(out=outst[:], in0=r2_sb[:], scalar1=mu_n[:, 0:1],
                                               scalar2=inv_s[:, 0:1], op0=ALU.add, op1=ALU.mult))
        t_oe = inc("vector", "SV")
        t_x1_use = t_oe
        wait("gpsimd", "SV", t_oe)
        t_out_st = dma("gpsimd", "GOUT", dout.ap()[j * CH:(j + 1) * CH, :], outst[:])

    wait("gpsimd", "GOUT", t_out_st)

    return _materialize(nc, prog, semh, sem_names)


def make_in_maps(x, wt, gwt, vwt, owt, lam, eta, gam, bet, gb, vb, ob):
    ones1 = np.ones((1, 128), dtype=np.float32)
    ii = np.arange(128, dtype=np.float32)
    dmat_h = ii[None, :] - ii[:, None]
    ltm_h = (dmat_h >= 0).astype(np.float32)
    dmat_h = np.where(dmat_h >= 0, dmat_h, 0.0).astype(np.float32)
    idn_h = np.eye(128, dtype=np.float32)
    iotap1 = np.broadcast_to(ii[None, :] + 1.0, (4, 128)).astype(np.float32).copy()
    iotarev = (127.0 - ii)[:, None].astype(np.float32).copy()
    c128 = np.full((4, 1), 128.0, dtype=np.float32)
    xprev = np.ascontiguousarray(x[0:PT])

    in_maps = []
    for c in range(NCORE):
        t0 = c * TC
        s_idx = ii[:, None] + 128.0 * np.arange(NPT, dtype=np.float32)[None, :]
        pmask = (s_idx < t0).astype(np.float32)
        pexp = np.where(pmask > 0, (t0 - 1.0) - s_idx, 0.0).astype(np.float32)
        in_maps.append({
            "xc": np.ascontiguousarray(x[t0:t0 + TC]),
            "xprev": xprev,
            "wt": wt, "gwt": gwt, "vwt": vwt, "owt": owt,
            "lam_logit": lam, "eta": eta, "gam": gam, "bet": bet,
            "gb": gb, "vb": vb, "ob": ob,
            "ones1": ones1, "dmat": dmat_h, "ltm": ltm_h, "idn": idn_h,
            "iotap1": iotap1, "iotarev": iotarev, "c128": c128,
            "pexp": np.ascontiguousarray(pexp), "pmask": np.ascontiguousarray(pmask),
        })
    return in_maps


def kernel(L0_attn_x, L0_attn_W, L0_attn_lam_logit, L0_attn_eta, L0_attn_gamma,
           L0_attn_beta, L0_ffn_gate_w, L0_ffn_gate_b, L0_ffn_val_w, L0_ffn_val_b,
           L0_ffn_out_w, L0_ffn_out_b):
    x = np.ascontiguousarray(np.asarray(L0_attn_x, dtype=np.float32).reshape(T, D))
    wt = np.ascontiguousarray(np.asarray(L0_attn_W, dtype=np.float32).T)
    gwt = np.ascontiguousarray(np.asarray(L0_ffn_gate_w, dtype=np.float32).T)
    vwt = np.ascontiguousarray(np.asarray(L0_ffn_val_w, dtype=np.float32).T)
    owt = np.ascontiguousarray(np.asarray(L0_ffn_out_w, dtype=np.float32).T)
    lam = np.asarray(L0_attn_lam_logit, dtype=np.float32).reshape(1, 4)
    eta = np.asarray(L0_attn_eta, dtype=np.float32).reshape(1, 4)
    gam = np.asarray(L0_attn_gamma, dtype=np.float32).reshape(1, D)
    bet = np.asarray(L0_attn_beta, dtype=np.float32).reshape(1, D)
    gb = np.asarray(L0_ffn_gate_b, dtype=np.float32).reshape(1, H)
    vb = np.asarray(L0_ffn_val_b, dtype=np.float32).reshape(1, H)
    ob = np.asarray(L0_ffn_out_b, dtype=np.float32).reshape(1, D)

    in_maps = make_in_maps(x, wt, gwt, vwt, owt, lam, eta, gam, bet, gb, vb, ob)
    nc = build_program()
    res = run_bass_kernel_spmd(nc, in_maps, core_ids=list(range(NCORE)))
    out = np.concatenate([np.asarray(r["out"]) for r in res.results], axis=0)
    return out.reshape(1, T, D).astype(np.float32)


# version 04 (final)
# speedup vs baseline: 1.0375x; 1.0375x over previous
import numpy as np

import concourse.bass as bass
import concourse.mybir as mybir
from concourse.bass_utils import run_bass_kernel_spmd

F32 = mybir.dt.float32
BF16 = mybir.dt.bfloat16
I32 = mybir.dt.int32
ALU = mybir.AluOpType
AF = mybir.ActivationFunctionType
AX = mybir.AxisListType

NCORE = 8
T = 8192
TC = T // NCORE
D = 2048
H = 8192
CH = 128
NCHUNK = TC // CH
PT = T - TC
NPT = PT // CH
HB = 512
NHB = H // HB
LN_EPS = 1e-5
RB = 384.0



def _materialize(nc, prog, semh, sem_names):
    import contextlib
    with contextlib.ExitStack() as stack:
        for n in sem_names:
            semh[n] = stack.enter_context(nc.semaphore(n))
        block = stack.enter_context(nc.Block())

        @block.gpsimd
        def _(e):
            for f in prog["gpsimd"]:
                f(e)

        @block.sync
        def _(e):
            for f in prog["sync"]:
                f(e)

        @block.scalar
        def _(e):
            for f in prog["scalar"]:
                f(e)

        @block.vector
        def _(e):
            for f in prog["vector"]:
                f(e)

        @block.tensor
        def _(e):
            for f in prog["tensor"]:
                f(e)

    return nc

def build_program(limit=99):
    import os
    nc = bass.Bass("TRN2", target_bir_lowering=False, detect_race_conditions=bool(int(os.environ.get("KRC", "0"))))

    dx = nc.dram_tensor("xc", [TC, D], F32, kind="ExternalInput")
    dxprev = nc.dram_tensor("xprev", [PT, D], F32, kind="ExternalInput")
    dwt = nc.dram_tensor("wt", [D, D], F32, kind="ExternalInput")
    dgwt = nc.dram_tensor("gwt", [D, H], F32, kind="ExternalInput")
    dvwt = nc.dram_tensor("vwt", [D, H], F32, kind="ExternalInput")
    dowt = nc.dram_tensor("owt", [H, D], F32, kind="ExternalInput")
    dlam = nc.dram_tensor("lam_logit", [1, 4], F32, kind="ExternalInput")
    deta = nc.dram_tensor("eta", [1, 4], F32, kind="ExternalInput")
    dgam = nc.dram_tensor("gam", [1, D], F32, kind="ExternalInput")
    dbet = nc.dram_tensor("bet", [1, D], F32, kind="ExternalInput")
    dgb = nc.dram_tensor("gb", [1, H], F32, kind="ExternalInput")
    dvb = nc.dram_tensor("vb", [1, H], F32, kind="ExternalInput")
    dob = nc.dram_tensor("ob", [1, D], F32, kind="ExternalInput")
    dones = nc.dram_tensor("ones1", [1, 128], F32, kind="ExternalInput")
    ddmat = nc.dram_tensor("dmat", [128, 128], F32, kind="ExternalInput")
    dltm = nc.dram_tensor("ltm", [128, 128], F32, kind="ExternalInput")
    didn = nc.dram_tensor("idn", [128, 128], F32, kind="ExternalInput")
    dip1 = nc.dram_tensor("iotap1", [4, 128], F32, kind="ExternalInput")
    direv = nc.dram_tensor("iotarev", [128, 1], F32, kind="ExternalInput")
    dc128 = nc.dram_tensor("c128", [4, 1], F32, kind="ExternalInput")
    dpexp = nc.dram_tensor("pexp", [128, NPT], F32, kind="ExternalInput")
    dpmask = nc.dram_tensor("pmask", [128, NPT], F32, kind="ExternalInput")

    dout = nc.dram_tensor("out", [TC, D], F32, kind="ExternalOutput")
    _sk = "ExternalOutput" if os.environ.get("KDBG") else "Internal"
    sxbf = nc.dram_tensor("s_xbf", [TC, D], BF16, kind=_sk)
    sxq = nc.dram_tensor("s_xq", [TC, D], BF16, kind=_sk)
    sgv = nc.dram_tensor("s_gv", [TC, H], BF16, kind=_sk)
    sgvq = nc.dram_tensor("s_gvq", [TC, H], BF16, kind=_sk)
    sx1 = nc.dram_tensor("s_x1", [TC, D], F32, kind=_sk)

    prog = {"tensor": [], "vector": [], "scalar": [], "gpsimd": [], "sync": []}
    from collections import defaultdict
    sems = defaultdict(int)
    semh = {}
    sem_names = []

    def _sem(name):
        if name not in sem_names:
            sem_names.append(name)
        return name

    auto_cnt = {"vector": 0, "scalar": 0}

    def op(eng, fn):
        if eng in auto_cnt:
            k = auto_cnt[eng]
            s = "AV" if eng == "vector" else "AA"
            _sem(s)
            if k > 0:
                prog[eng].append(lambda e, s=s, k=k: e.wait_ge(semh[s], k))

            def chained(e, f=fn, s=s):
                f(e).then_inc(semh[s], 1)

            prog[eng].append(chained)
            auto_cnt[eng] = k + 1
        else:
            prog[eng].append(fn)

    def op_raw(eng, fn):
        prog[eng].append(fn)

    COMPUTE_SEMS = {"ST", "SV", "SA", "AV", "AA"}

    def wait(eng, sem, val):
        _sem(sem)
        if sem not in COMPUTE_SEMS:
            val = max(val, sems[sem]) if val > 0 else val
        if val > 0:
            op_raw(eng, lambda e, s=sem, v=val: e.wait_ge(semh[s], v))

    def bump(sem, n):
        sems[sem] += n
        return sems[sem]

    def dma(eng, sem, out, in_):
        _sem(sem)
        tok = bump(sem, 16)
        op_raw(eng, lambda e, o=out, i=in_, s=sem: e.dma_start(out=o, in_=i).then_inc(semh[s], 16))
        return tok

    def dmaT(eng, sem, out, in_):
        _sem(sem)
        tok = bump(sem, 16)
        op_raw(eng, lambda e, o=out, i=in_, s=sem: e.dma_start_transpose(out=o, in_=i).then_inc(semh[s], 16))
        return tok

    def inc(eng, sem):
        _sem(sem)
        tok = bump(sem, 1)
        if eng in auto_cnt:
            s_self = "AV" if eng == "vector" else "AA"
            k = auto_cnt[eng]
            op_raw(eng, lambda e, ss=s_self, k=k: e.wait_ge(semh[ss], k))
            op_raw(eng, lambda e, s=sem: e.sem_inc(semh[s], 1))
        else:
            prev = prog[eng].pop()

            def combined(e, p=prev, s=sem):
                p(e).then_inc(semh[s], 1)

            prog[eng].append(combined)
        return tok

    def sb(name, shape, dtype):
        return nc.alloc_sbuf_tensor(name, shape, dtype)

    ones_bf = sb("ones_bf", [1, 128], BF16)
    ones128 = sb("ones128", [128, 128], BF16)
    idn_f = sb("idn_f", [128, 128], F32)
    dmat_f = sb("dmat_f", [128, 128], F32)
    ltm_f = sb("ltm_f", [128, 128], F32)
    ip1_f = sb("ip1_f", [4, 128], F32)
    irev_f = sb("irev_f", [128, 1], F32)
    c128_f = sb("c128_f", [4, 1], F32)
    pexp_f = sb("pexp_f", [128, NPT], F32)
    pmask_f = sb("pmask_f", [128, NPT], F32)
    lamv = sb("lamv", [1, 16], F32)
    hilo = sb("hilo", [1, 16], BF16)
    llb = sb("llb", [128, 36], F32)
    tpA = sb("tpA", [32, 32], F32)
    tpB = sb("tpB", [32, 32], F32)
    tpC = sb("tpC", [32, 32], F32)
    tpD = sb("tpD", [32, 32], F32)
    lamC = sb("lamC", [4, 1], F32)
    gt_f = sb("gt_f", [128, 128], F32)
    gt_bf = sb("gt_bf", [128, 128], BF16)
    pt_bf = sb("pt_bf", [4, 128], BF16)
    qt_f = sb("qt_f", [128, 4], F32)
    qt_bf = sb("qt_bf", [128, 4], BF16)
    vt_f = sb("vt_f", [128, NPT * 4], F32)
    vt_bf = sb("vt_bf", [128, NPT * 4], BF16)
    ypad_f = sb("ypad_f", [32, D], F32)
    ypad_bf = sb("ypad_bf", [32, D], BF16)
    yt_bf = sb("yt_bf", [128, 16 * 32], BF16)
    c_f = sb("c_f", [4, D], F32)
    c_bf = sb("c_bf", [4, D], BF16)
    stat6 = sb("stat6", [128, 24], F32)
    stat2 = sb("stat2", [128, 2], F32)
    mu_n = sb("mu_n", [128, 1], F32)
    inv_s = sb("inv_s", [128, 1], F32)
    red1 = sb("red1", [128, 1], F32)
    g127 = sb("g127", [128, NCHUNK], F32)
    sg_a = sb("sg_a", [128, NCHUNK], F32)
    sv_a = sb("sv_a", [128, NCHUNK], F32)
    qs_a = sb("qs_a", [128, NCHUNK], F32)
    s2_a = sb("s2_a", [128, NCHUNK], F32)
    gmax = sb("gmax", [128, NCHUNK], F32)
    gwsum = sb("gwsum", [128, 3], F32)
    acc96 = sb("acc96", [128, 192], F32)
    gshlb = sb("gshlb", [128, 8], BF16)
    gwsc = sb("gwsc", [128, 8], F32)
    negsc = sb("negsc", [128, 4], F32)
    prow = sb("prow", [1, 16], BF16)
    tiny = sb("tiny", [4, 8], F32)
    rowf = sb("rowf", [1, 16], F32)
    gshf = sb("gshf", [128, 8], F32)

    POOL_BYTES = 180224
    POOL_BYTES = 179392
    pool = nc.alloc_sbuf_tensor("pool", [128, POOL_BYTES], mybir.dt.uint8)
    pool_base = nc.lookup_mloc(pool).addr

    class Ph:
        def __init__(self):
            self.off = 0

        def t(self, name, shape, dtype):
            nb = int(np.prod(shape[1:])) * mybir.dt.size(dtype)
            off = self.off
            self.off = (off + nb + 31) & ~31
            assert self.off <= POOL_BYTES, (name, self.off)
            return nc.alloc_sbuf_tensor_at(name, shape, dtype, offset=pool_base + off)

    A = Ph()
    wtb = A.t("wtb", [128, 16 * D], BF16)
    xT = A.t("xT", [128, 16 * TC], BF16)
    z2 = A.t("z2", [128, 2 * D], BF16)
    gam_b = A.t("gam_b", [128, D], F32)
    bet_b = A.t("bet_b", [128, D], F32)
    t_s1 = A.t("t_s1", [128, D], F32)
    u_s1 = A.t("u_s1", [128, D], F32)
    xf32 = A.t("xf32", [128, D], F32)
    xpc = A.t("xpc", [128, 2 * D], BF16)
    x1_c = A.t("x1_c", [128, D], F32)
    xqi = A.t("xqi", [128, D], I32)
    xqb = A.t("xqb", [128, D], BF16)

    B = Ph()
    xqT = B.t("xqT", [128, 16 * TC], BF16)
    gb_sl = B.t("gb_sl", [128, 2 * HB], F32)
    vb_sl = B.t("vb_sl", [128, 2 * HB], F32)
    wland = B.t("wland", [128, 2 * 2048], F32)
    gq_s = B.t("gq_s", [128, 2 * 16 * HB], BF16)
    vq_s = B.t("vq_s", [128, 2 * 16 * HB], BF16)
    wqi = B.t("wqi", [128, 2048], I32)
    gvst = B.t("gvst", [128, 8 * HB], BF16)
    gf_s = B.t("gf_s", [128, HB], F32)
    vf_s = B.t("vf_s", [128, HB], F32)
    gvc = B.t("gvc", [128, 2 * 4096], BF16)
    gvqi = B.t("gvqi", [128, 4096], I32)

    C = Ph()
    oacc = C.t("oacc", [128, NCHUNK * D], F32)
    owts = C.t("owts", [128, 2 * 8 * D], BF16)
    owl = C.t("owl", [128, D], F32)
    gvqT = C.t("gvqT", [128, 8 * TC], BF16)
    obb = C.t("obb", [128, D], F32)
    x1l = C.t("x1l", [128, D], F32)
    outst = x1l
    r2_sb = owl

    psA = nc.alloc_psum_tensor("psA", [128, 2048], F32)
    psB = nc.alloc_psum_tensor("psB", [128, 2048], F32)

    op("vector", lambda e: e.memset(llb[:, 8:36], 0.0))
    op("vector", lambda e: e.memset(tpC[:], 0.0))
    op("vector", lambda e: e.memset(tpD[:], 0.0))
    op("vector", lambda e: e.memset(prow[:], 0.0))
    inc("vector", "SV")

    def finish(nc_, prog_, semh_, sems_):
        dout_ap = dout.ap()
        _sem("GOUT")
        for n_ in list(sem_names):
            if sems_[n_] > 0:
                prog_["gpsimd"].append(lambda e, n_=n_, v=sems_[n_]: e.wait_ge(semh_[n_], v))
        n = sems_["GOUT"]
        for j in range(NCHUNK):
            n += 16
            prog_["gpsimd"].append(lambda e, j=j: e.dma_start(
                out=dout_ap[j * CH:(j + 1) * CH, :], in_=outst[:]).then_inc(semh_["GOUT"], 16))
        prog_["gpsimd"].append(lambda e, n=n: e.wait_ge(semh_["GOUT"], n))
        prog_["vector"].append(lambda e: e.memset(outst[:], 0.0))
        return _materialize(nc_, prog_, semh_, sem_names)

    def ln_stats(src):
        for i in range(4):
            op("vector", lambda e, i=i, s=src: e.bn_stats(out=stat6[:, i * 6:(i + 1) * 6],
                                                          in_=s[:, i * 512:(i + 1) * 512]))
        op("vector", lambda e: e.bn_aggr(out=stat2[:], in_=stat6[:]))
        op("vector", lambda e: e.tensor_scalar_mul(out=mu_n[:], in0=stat2[:, 0:1], scalar1=-1.0))
        op("vector", lambda e: e.tensor_scalar_add(out=red1[:], in0=stat2[:, 1:2], scalar1=LN_EPS))
        t = inc("vector", "SV")
        wait("scalar", "SV", t)
        op("scalar", lambda e: e.activation(out=inv_s[:], in_=red1[:], func=AF.Sqrt))
        t2 = inc("scalar", "SA")
        wait("vector", "SA", t2)
        op("vector", lambda e: e.reciprocal(out=inv_s[:], in_=inv_s[:]))

    t_consts = 0
    for (dst, src) in [(idn_f, didn), (dmat_f, ddmat), (ltm_f, dltm),
                       (ip1_f, dip1), (irev_f, direv), (c128_f, dc128),
                       (pexp_f, dpexp), (pmask_f, dpmask)]:
        t_consts = dma("sync", "YC", dst[:], src.ap())
    t_yc = dma("sync", "YC", lamv[:, 0:4], dlam.ap())
    t_yc = dma("sync", "YC", lamv[:, 4:8], deta.ap())
    t_gb_bc = dma("sync", "YGB", gam_b[:], dgam.ap().partition_broadcast(128))
    t_bb_bc = dma("sync", "YGB", bet_b[:], dbet.ap().partition_broadcast(128))
    t_ones = dma("gpsimd", "GON", ones_bf[:], dones.ap())
    t_wt = 0
    for dk in range(16):
        t_wt = dma("gpsimd", "GW", wtb[:, dk * D:(dk + 1) * D],
                   dwt.ap()[dk * 128:(dk + 1) * 128, :])

    xb_st = [0] * NCHUNK
    for j in range(NCHUNK):
        sl = xpc[:, (j % 2) * D:(j % 2) * D + D]
        if j >= 2:
            wait("gpsimd", "GXS", xb_st[j - 2])
        t_l = dma("gpsimd", "GX", sl, dx.ap()[j * CH:(j + 1) * CH, :])
        wait("gpsimd", "GX", t_l)
        xb_st[j] = dma("gpsimd", "GXS", sxbf.ap()[j * CH:(j + 1) * CH, :], sl)
    t_xbf_w = xb_st[-1]
    wait("sync", "GXS", t_xbf_w)
    t_xT = 0
    for dk in range(16):
        t_xT = dmaT("sync", "YXT", xT[:, dk * TC:(dk + 1) * TC],
                    sxbf.ap()[:, dk * 128:(dk + 1) * 128])

    if limit < 1:
        return finish(nc, prog, semh, sems)
    wait("scalar", "YC", t_yc)
    op("scalar", lambda e: e.activation(out=lamv[:, 8:12], in_=lamv[:, 0:4], func=AF.Sigmoid))
    t = inc("scalar", "SA")
    wait("vector", "SA", t)
    op("vector", lambda e: e.tensor_scalar_max(out=lamv[:, 8:12], in0=lamv[:, 8:12], scalar1=1e-38))
    t = inc("vector", "SV")
    wait("scalar", "SV", t)
    op("scalar", lambda e: e.activation(out=lamv[:, 12:16], in_=lamv[:, 8:12], func=AF.Ln))
    t = inc("scalar", "SA")
    wait("vector", "SA", t)
    op("vector", lambda e: e.tensor_copy(out=hilo[:, 0:4], in_=lamv[:, 12:16]))
    op("vector", lambda e: e.tensor_copy(out=hilo[:, 4:8], in_=lamv[:, 4:8]))
    op("vector", lambda e: e.tensor_copy(out=rowf[0:1, 0:4], in_=hilo[:, 0:4]))
    op("vector", lambda e: e.tensor_copy(out=rowf[0:1, 4:8], in_=hilo[:, 4:8]))
    op("vector", lambda e: e.tensor_sub(out=rowf[0:1, 8:12], in0=lamv[:, 12:16], in1=rowf[0:1, 0:4]))
    op("vector", lambda e: e.tensor_sub(out=rowf[0:1, 12:16], in0=lamv[:, 4:8], in1=rowf[0:1, 4:8]))
    op("vector", lambda e: e.tensor_copy(out=hilo[:, 8:16], in_=rowf[0:1, 8:16]))
    t_hilo = inc("vector", "SV")
    wait("tensor", "GON", t_ones)
    wait("tensor", "SV", t_hilo)
    op("tensor", lambda e: e.matmul(psB[:, 0:8], ones_bf[:], hilo[:, 0:8], start=True, stop=False))
    op("tensor", lambda e: e.matmul(psB[:, 0:8], ones_bf[:], hilo[:, 8:16], start=False, stop=True))
    op("tensor", lambda e: e.matmul(psA[:, 0:128], ones_bf[:], ones_bf[:], start=True, stop=True))
    t_bc1 = inc("tensor", "ST")
    wait("vector", "ST", t_bc1)
    op("vector", lambda e: e.tensor_copy(out=llb[:, 0:8], in_=psB[:, 0:8]))
    op("vector", lambda e: e.tensor_copy(out=ones128[:], in_=psA[:, 0:128]))
    op("vector", lambda e: e.transpose(out=tpA[:], in_=llb[0:32, 0:32]))
    op("vector", lambda e: e.transpose(out=tpB[:], in_=llb[0:32, 4:36]))
    t_llp = inc("vector", "SV")
    ll_p = tpA
    eta_p = tpB

    wait("scalar", "YC", t_yc)
    wait("scalar", "SV", t_llp)
    op("vector", lambda e: e.memset(gt_f[:], 0.0))
    t_g0 = inc("vector", "SV")
    for k in range(4):
        op("scalar", lambda e, k=k: e.activation(out=u_s1[:, 0:128], in_=dmat_f[:], func=AF.Exp,
                                                 scale=llb[:, k:k + 1]))
        t_e = inc("scalar", "SA")
        wait("vector", "SA", t_e)
        if k == 0:
            wait("vector", "SV", t_g0)
        op("vector", lambda e, k=k: e.scalar_tensor_tensor(out=gt_f[:], in0=u_s1[:, 0:128],
                                                           scalar=llb[:, 4 + k:5 + k], in1=gt_f[:],
                                                           op0=ALU.mult, op1=ALU.add))
        t_gk = inc("vector", "SV")
        wait("scalar", "SV", t_gk)
    op("vector", lambda e: e.tensor_mul(out=gt_f[:], in0=gt_f[:], in1=ltm_f[:]))
    op("vector", lambda e: e.tensor_add(out=gt_f[:], in0=gt_f[:], in1=idn_f[:]))
    op("vector", lambda e: e.tensor_copy(out=gt_bf[:], in_=gt_f[:]))
    t_gt = inc("vector", "SV")

    op("scalar", lambda e: e.activation(out=ypad_f[0:4, 0:128], in_=ip1_f[:], func=AF.Exp,
                                        scale=ll_p[0:4, 0:1]))
    t = inc("scalar", "SA")
    wait("vector", "SA", t)
    op("vector", lambda e: e.tensor_scalar_mul(out=ypad_f[0:4, 0:128], in0=ypad_f[0:4, 0:128],
                                               scalar1=eta_p[0:4, 0:1]))
    op("vector", lambda e: e.tensor_copy(out=pt_bf[:], in_=ypad_f[0:4, 0:128]))
    t_ptb = inc("vector", "SV")
    for k in range(4):
        op("scalar", lambda e, k=k: e.activation(out=qt_f[:, k:k + 1], in_=irev_f[:], func=AF.Exp,
                                                 scale=llb[:, k:k + 1]))
    op("scalar", lambda e: e.activation(out=lamC[:], in_=c128_f[:], func=AF.Exp, scale=ll_p[0:4, 0:1]))
    for k in range(4):
        op("scalar", lambda e, k=k: e.activation(out=vt_f[:, k:NPT * 4:4], in_=pexp_f[:], func=AF.Exp,
                                                 scale=llb[:, k:k + 1]))
    t_acts = inc("scalar", "SA")
    wait("vector", "SA", t_acts)
    op("vector", lambda e: e.tensor_copy(out=qt_bf[:], in_=qt_f[:]))
    for k in range(4):
        op("vector", lambda e, k=k: e.tensor_mul(out=vt_f[:, k:NPT * 4:4], in0=vt_f[:, k:NPT * 4:4],
                                                 in1=pmask_f[:]))
    op("vector", lambda e: e.tensor_copy(out=vt_bf[:], in_=vt_f[:]))
    t_vtb = inc("vector", "SV")

    if limit < 2:
        return finish(nc, prog, semh, sems)
    wait("tensor", "SV", t_vtb)
    mm_y_toks = [0] * NPT
    for st in range(NPT):
        sl = xpc[:, (st % 2) * D:(st % 2) * D + D]
        if st < 2:
            wait("gpsimd", "GXS", t_xbf_w)
        else:
            wait("gpsimd", "ST", mm_y_toks[st - 2])
        t_l = dma("gpsimd", "GP%d" % (st % 2), sl, dxprev.ap()[st * CH:(st + 1) * CH, :])
        wait("tensor", "GP%d" % (st % 2), t_l)
        for q in range(4):
            cs = slice(q * 512, q * 512 + 512)
            op("tensor", lambda e, st=st, sl=sl, cs=cs:
               e.matmul(psA[0:4, cs], vt_bf[:, st * 4:(st + 1) * 4], sl[:, cs],
                        start=(st == 0), stop=(st == NPT - 1)))
        mm_y_toks[st] = inc("tensor", "ST")
    mm_y = mm_y_toks[-1]
    if limit < 1.3:
        return finish(nc, prog, semh, sems)
    op("vector", lambda e: e.memset(ypad_f[:], 0.0))
    t_ymz = inc("vector", "SV")
    wait("scalar", "ST", mm_y)
    wait("scalar", "SV", t_ymz)
    op("scalar", lambda e: e.copy(out=ypad_f[0:4, :], in_=psA[0:4, :]))
    t = inc("scalar", "SA")
    wait("vector", "SA", t)
    op("vector", lambda e: e.tensor_copy(out=ypad_bf[:], in_=ypad_f[:]))
    for dk in range(16):
        for b in range(4):
            op("vector", lambda e, dk=dk, b=b:
               e.transpose(out=yt_bf[32 * b:32 * b + 32, dk * 32:dk * 32 + 32],
                           in_=ypad_bf[0:32, dk * 128 + 32 * b:dk * 128 + 32 * b + 32]))
    t_yt = inc("vector", "SV")
    if limit < 1.7:
        return finish(nc, prog, semh, sems)
    wait("tensor", "SV", t_yt)
    wait("tensor", "GW", t_wt)
    for dk in range(16):
        for q in range(4):
            cs = slice(q * 512, q * 512 + 512)
            op("tensor", lambda e, dk=dk, cs=cs:
               e.matmul(psB[0:4, cs], yt_bf[:, dk * 32:dk * 32 + 4],
                        wtb[:, dk * D + cs.start:dk * D + cs.stop],
                        start=(dk == 0), stop=(dk == 15)))
    mm_c = inc("tensor", "ST")
    if limit < 1.9:
        return finish(nc, prog, semh, sems)
    wait("scalar", "ST", mm_c)
    op("scalar", lambda e: e.copy(out=c_f[:], in_=psB[0:4, :]))
    t = inc("scalar", "SA")
    wait("vector", "SA", t)
    op("vector", lambda e: e.tensor_copy(out=c_bf[:], in_=c_f[:]))
    t_cbf = inc("vector", "SV")
    if limit < 1.95:
        return finish(nc, prog, semh, sems)

    if limit < 2:
        return finish(nc, prog, semh, sems)
    wait("tensor", "YXT", t_xT)
    wait("tensor", "SA", sems["SA"])
    t_sev = 0
    t_hfree = 0
    t_zuse = [0, 0]
    t_x1w = 0
    t_xqw = 0
    t_xf_use = 0
    mm_h_tok = [0] * NCHUNK
    first_ep = True
    _nch = 1 if limit < 2.08 else (2 if limit < 2.1 else NCHUNK)
    for j in range(_nch):
        slz = z2[:, (j % 2) * D:(j % 2) * D + D]
        if j >= 1:
            wait("sync", "SV", t_xf_use)
        t_xf = dma("sync", "YXF", xf32[:], dx.ap()[j * CH:(j + 1) * CH, :])

        wait("tensor", "SV", t_hfree)
        for dk in range(16):
            for q in range(4):
                cs = slice(q * 512, q * 512 + 512)
                op("tensor", lambda e, j=j, dk=dk, cs=cs:
                   e.matmul(psA[:, cs], xT[:, dk * TC + j * CH: dk * TC + (j + 1) * CH],
                            wtb[:, dk * D + cs.start:dk * D + cs.stop],
                            start=(dk == 0), stop=(dk == 15)))
        mm_z = inc("tensor", "ST")
        wait("scalar", "ST", mm_z)
        if j >= 2:
            wait("scalar", "ST", mm_h_tok[j - 2])
        op("scalar", lambda e, slz=slz: e.copy(out=slz, in_=psA[:]))
        t_zev = inc("scalar", "SA")
        if limit < 2.3:
            t_hfree = 0
            wait("tensor", "SA", t_zev)
            continue

        wait("tensor", "SV", t_sev)
        wait("tensor", "SA", t_zev)
        if j == 0:
            wait("tensor", "SV", t_cbf)
            wait("tensor", "SV", t_gt)
            wait("tensor", "SV", t_ptb)
        for q in range(4):
            cs = slice(q * 512, q * 512 + 512)
            op("tensor", lambda e, slz=slz, cs=cs:
               e.matmul(psB[0:4, cs], qt_bf[:], slz[:, cs], start=True, stop=True))
        mm_s = inc("tensor", "ST")
        for q in range(4):
            cs = slice(q * 512, q * 512 + 512)
            op("tensor", lambda e, slz=slz, cs=cs:
               e.matmul(psA[:, cs], gt_bf[:], slz[:, cs], start=True, stop=False))
        for q in range(4):
            cs = slice(q * 512, q * 512 + 512)
            op("tensor", lambda e, cs=cs:
               e.matmul(psA[:, cs], pt_bf[:], c_bf[:, cs], start=False, stop=True))
        mm_h = inc("tensor", "ST")
        mm_h_tok[j] = mm_h
        wait("vector", "ST", mm_s)
        op("vector", lambda e: e.scalar_tensor_tensor(out=c_f[:], in0=c_f[:], scalar=lamC[:, 0:1],
                                                      in1=psB[0:4, :], op0=ALU.mult, op1=ALU.add))
        op("vector", lambda e: e.tensor_copy(out=c_bf[:], in_=c_f[:]))
        t_sev = inc("vector", "SV")
        if limit < 2.5:
            wait("vector", "ST", mm_h)
            op("vector", lambda e: e.tensor_copy(out=t_s1[:], in_=psA[:]))
            t_hfree = inc("vector", "SV")
            continue
        wait("vector", "ST", mm_h)
        if first_ep:
            wait("vector", "YGB", t_bb_bc)
            first_ep = False
        ln_stats(psA)
        op("vector", lambda e: e.tensor_scalar(out=t_s1[:], in0=psA[:], scalar1=mu_n[:, 0:1],
                                               scalar2=inv_s[:, 0:1], op0=ALU.add, op1=ALU.mult))
        t_hfree = inc("vector", "SV")
        op("vector", lambda e: e.tensor_mul(out=u_s1[:], in0=t_s1[:], in1=gam_b[:]))
        wait("vector", "YXF", t_xf)
        op("vector", lambda e: e.tensor_add(out=t_s1[:], in0=xf32[:], in1=bet_b[:]))
        t_xf_use = inc("vector", "SV")
        op("vector", lambda e: e.tensor_add(out=u_s1[:], in0=u_s1[:], in1=t_s1[:]))
        ln_stats(u_s1)
        op("vector", lambda e: e.tensor_scalar(out=x1_c[:], in0=u_s1[:], scalar1=mu_n[:, 0:1],
                                               scalar2=inv_s[:, 0:1], op0=ALU.add, op1=ALU.mult))
        op("vector", lambda e: e.tensor_reduce(out=red1[:], in_=x1_c[:], axis=AX.X,
                                               op=ALU.max, apply_absolute_value=True))
        op("vector", lambda e: e.tensor_scalar_max(out=red1[:], in0=red1[:], scalar1=1e-5))
        op("vector", lambda e, j=j: e.tensor_scalar_mul(out=g127[:, j:j + 1], in0=red1[:],
                                                        scalar1=1.0 / 127.0))
        op("vector", lambda e: e.reciprocal(out=red1[:], in_=red1[:]))
        op("vector", lambda e: e.tensor_scalar_mul(out=red1[:], in0=red1[:], scalar1=127.0))
        if j >= 1:
            wait("vector", "GX1", t_x1w)
            wait("vector", "GXQ", t_xqw)
        op("vector", lambda e: e.tensor_scalar(out=u_s1[:], in0=x1_c[:], scalar1=red1[:, 0:1],
                                               scalar2=RB + 0.5, op0=ALU.mult, op1=ALU.add))
        op("vector", lambda e: e.tensor_copy(out=xqi[:], in_=u_s1[:]))
        op("vector", lambda e: e.tensor_tensor(out=t_s1[:], in0=xqi[:], in1=u_s1[:],
                                               op=ALU.is_gt))
        op("vector", lambda e: e.scalar_tensor_tensor(out=xqb[:], in0=xqi[:], scalar=-RB,
                                                      in1=t_s1[:], op0=ALU.add,
                                                      op1=ALU.subtract))
        t_q = inc("vector", "SV")
        wait("gpsimd", "SV", t_q)
        t_x1w = dma("gpsimd", "GX1", sx1.ap()[j * CH:(j + 1) * CH, :], x1_c[:])
        t_xqw = dma("gpsimd", "GXQ", sxq.ap()[j * CH:(j + 1) * CH, :], xqb[:])

    if limit < 3:
        return finish(nc, prog, semh, sems)
    tile_list = []
    for (src, ntile, freew, m) in [(dgwt, 16, H, 0), (dvwt, 16, H, 1), (dowt, 64, D, 2)]:
        nsub = freew // 2048
        for i in range(ntile):
            for s in range(nsub):
                tile_list.append((src, i, s, m))
    t_gred = [0] * len(tile_list)
    for idx, (src, i, s, m) in enumerate(tile_list):
        slot = idx % 2
        sl = wland[:, slot * 2048:(slot + 1) * 2048]
        if idx >= 2:
            wait("sync", "AA", t_gred[idx - 2])
        else:
            wait("sync", "ST", mm_h_tok[-1])
        tok = dma("sync", "CG%d" % (idx % 2), sl,
                  src.ap()[i * 128:(i + 1) * 128, s * 2048:(s + 1) * 2048])
        wait("scalar", "CG%d" % (idx % 2), tok)
        op("scalar", lambda e, sl=sl, idx=idx: e.activation(
            out=sl, in_=sl, func=AF.Abs, accum_out=acc96[:, idx:idx + 1]))
        t_gred[idx] = auto_cnt["scalar"]
    t_gacc = inc("scalar", "SA")
    wait("vector", "SA", t_gacc)
    op("vector", lambda e: e.tensor_reduce(out=gwsum[:, 0:1], in_=acc96[:, 0:64], axis=AX.X,
                                           op=ALU.add))
    op("vector", lambda e: e.tensor_reduce(out=gwsum[:, 1:2], in_=acc96[:, 64:128], axis=AX.X,
                                           op=ALU.add))
    op("vector", lambda e: e.tensor_reduce(out=gwsum[:, 2:3], in_=acc96[:, 128:192], axis=AX.X,
                                           op=ALU.add))
    op("vector", lambda e: e.tensor_copy(out=gshlb[:, 0:3], in_=gwsum[:]))
    op("vector", lambda e: e.tensor_copy(out=gshf[:, 0:3], in_=gshlb[:, 0:3]))
    op("vector", lambda e: e.tensor_sub(out=gshf[:, 4:7], in0=gwsum[:], in1=gshf[:, 0:3]))
    op("vector", lambda e: e.tensor_copy(out=gshlb[:, 4:7], in_=gshf[:, 4:7]))
    t_hl = inc("vector", "SV")
    wait("tensor", "SV", t_hl)
    op("tensor", lambda e: e.matmul(psB[0:3, 0:128], gshlb[:, 0:3], ones128[:], start=True, stop=False))
    op("tensor", lambda e: e.matmul(psB[0:3, 0:128], gshlb[:, 4:7], ones128[:], start=False, stop=True))
    t_tot = inc("tensor", "ST")
    wait("vector", "ST", t_tot)
    op("vector", lambda e: e.tensor_copy(out=tiny[0:3, 0:1], in_=psB[0:3, 0:1]))
    op("vector", lambda e: e.tensor_scalar_mul(out=tiny[0:3, 1:2], in0=tiny[0:3, 0:1],
                                               scalar1=1.0 / float(D * H)))
    op("vector", lambda e: e.tensor_scalar_max(out=tiny[0:3, 1:2], in0=tiny[0:3, 1:2], scalar1=1e-5))
    op("vector", lambda e: e.reciprocal(out=tiny[0:3, 2:3], in_=tiny[0:3, 1:2]))
    op("vector", lambda e: e.tensor_copy(out=tpC[0:3, 0:1], in_=tiny[0:3, 1:2]))
    op("vector", lambda e: e.tensor_copy(out=tpD[0:3, 0:1], in_=tiny[0:3, 2:3]))
    op("vector", lambda e: e.transpose(out=tpA[:], in_=tpC[:]))
    op("vector", lambda e: e.transpose(out=tpB[:], in_=tpD[:]))
    op("vector", lambda e: e.tensor_copy(out=rowf[0:1, 0:3], in_=tpA[0:1, 0:3]))
    op("vector", lambda e: e.tensor_copy(out=rowf[0:1, 3:6], in_=tpB[0:1, 0:3]))
    op("vector", lambda e: e.tensor_copy(out=prow[:, 0:6], in_=rowf[0:1, 0:6]))
    op("vector", lambda e: e.tensor_copy(out=rowf[0:1, 8:14], in_=prow[:, 0:6]))
    op("vector", lambda e: e.tensor_sub(out=lamv[:, 0:6], in0=rowf[0:1, 0:6], in1=rowf[0:1, 8:14]))
    op("vector", lambda e: e.tensor_copy(out=prow[:, 8:14], in_=lamv[:, 0:6]))
    t_ghl = inc("vector", "SV")
    wait("tensor", "SV", t_ghl)
    op("tensor", lambda e: e.matmul(psA[:, 0:8], ones_bf[:], prow[:, 0:8], start=True, stop=False))
    op("tensor", lambda e: e.matmul(psA[:, 0:8], ones_bf[:], prow[:, 8:16], start=False, stop=True))
    t_gw2 = inc("tensor", "ST")
    wait("vector", "ST", t_gw2)
    op("vector", lambda e: e.tensor_copy(out=gwsc[:], in_=psA[:, 0:8]))
    for m in range(3):
        op("vector", lambda e, m=m: e.tensor_scalar_mul(out=negsc[:, m:m + 1],
                                                        in0=gwsc[:, 3 + m:4 + m], scalar1=-1.0))
    for j in range(NCHUNK):
        op("vector", lambda e, j=j: e.tensor_mul(out=sg_a[:, j:j + 1], in0=g127[:, j:j + 1],
                                                 in1=gwsc[:, 0:1]))
        op("vector", lambda e, j=j: e.tensor_mul(out=sv_a[:, j:j + 1], in0=g127[:, j:j + 1],
                                                 in1=gwsc[:, 1:2]))
    t_scales = inc("vector", "SV")

    wait("sync", "GXQ", t_xqw)
    t_xqT = 0
    for dk in range(16):
        t_xqT = dmaT("sync", "YXQ", xqT[:, dk * TC:(dk + 1) * TC],
                     sxq.ap()[:, dk * 128:(dk + 1) * 128])

    if limit < 4:
        return finish(nc, prog, semh, sems)
    wq_slot_tok = [0, 0]
    wq_wqi_tok = [0]

    def emit_wq(hb, buf, src, invcol, ninvcol, t_prev_use):
        t_rdy2 = 0
        for span in range(4):
            slot = span % 2
            sl = wland[:, slot * 2048:(slot + 1) * 2048]
            wait("scalar", "SV", wq_slot_tok[slot])
            tok = dma("scalar", "C%d" % (span % 2), sl,
                      src.ap().rearrange("(b p) h -> p b h", p=128)
                      [:, span * 4:(span + 1) * 4, hb * HB:(hb + 1) * HB])
            wait("vector", "C%d" % (span % 2), tok)
            if t_prev_use:
                wait("vector", "ST", t_prev_use)
                t_prev_use = 0
            op("vector", lambda e, sl=sl, inv=invcol: e.tensor_scalar(
                out=wqi[:].bitcast(F32), in0=sl, scalar1=inv, scalar2=0.5,
                op0=ALU.mult, op1=ALU.is_ge))
            op("vector", lambda e, sl=sl, ninv=ninvcol: e.tensor_scalar(
                out=sl, in0=sl, scalar1=ninv, scalar2=0.5, op0=ALU.mult, op1=ALU.is_ge))
            op("vector", lambda e, sl=sl, buf=buf, span=span: e.tensor_sub(
                out=buf[:, span * 2048:(span + 1) * 2048], in0=wqi[:].bitcast(F32), in1=sl))
            t_rdy2 = inc("vector", "SV")
            wq_slot_tok[slot] = t_rdy2
        return t_rdy2

    wait("scalar", "AA", auto_cnt["scalar"])
    t_guse = [0, 0]
    t_vuse = [0, 0]
    t_gv_w = 0
    t_evac_v = {0: 0, 1: 0}
    t_evac_a = {0: 0, 1: 0}
    t_gvr = 0
    t_bsl_use = [0, 0]
    for hb in range(NHB):
        pbuf = hb % 2
        gbuf = gq_s[:, pbuf * 16 * HB:(pbuf + 1) * 16 * HB]
        vbuf = vq_s[:, pbuf * 16 * HB:(pbuf + 1) * 16 * HB]
        wait("sync", "SV", t_bsl_use[pbuf])
        t_gbs = dma("sync", "YB%d" % pbuf, gb_sl[:, pbuf * HB:(pbuf + 1) * HB],
                    dgb.ap()[0:1, hb * HB:(hb + 1) * HB].partition_broadcast(128))
        t_vbs = dma("sync", "YB%d" % pbuf, vb_sl[:, pbuf * HB:(pbuf + 1) * HB],
                    dvb.ap()[0:1, hb * HB:(hb + 1) * HB].partition_broadcast(128))
        if hb == 0:
            wait("vector", "SV", t_scales)
        t_gq = emit_wq(hb, gbuf, dgwt, gwsc[:, 3:4], negsc[:, 0:1], t_guse[pbuf])
        t_vq = emit_wq(hb, vbuf, dvwt, gwsc[:, 4:5], negsc[:, 1:2], t_vuse[pbuf])
        if hb == 0:
            wait("tensor", "YXQ", t_xqT)
        wait("tensor", "SV", t_gq)
        wait("tensor", "SV", t_vq)
        mm_g = mm_v = 0
        t_ev_last = 0
        for tt in range(8):
            pg = psA[:, (tt % 2) * 512:(tt % 2) * 512 + 512]
            pv = psB[:, (tt % 2) * 512:(tt % 2) * 512 + 512]
            wait("tensor", "SV", t_evac_v[tt % 2])
            wait("tensor", "SA", t_evac_a[tt % 2])
            for dk in range(16):
                op("tensor", lambda e, dk=dk, tt=tt, pg=pg, g=gbuf:
                   e.matmul(pg, xqT[:, dk * TC + tt * CH: dk * TC + (tt + 1) * CH],
                            g[:, dk * HB:(dk + 1) * HB], start=(dk == 0), stop=(dk == 15)))
            mm_g = inc("tensor", "ST")
            for dk in range(16):
                op("tensor", lambda e, dk=dk, tt=tt, pv=pv, v=vbuf:
                   e.matmul(pv, xqT[:, dk * TC + tt * CH: dk * TC + (tt + 1) * CH],
                            v[:, dk * HB:(dk + 1) * HB], start=(dk == 0), stop=(dk == 15)))
            mm_v = inc("tensor", "ST")
            wait("vector", "ST", mm_g)
            if tt == 0:
                wait("vector", "YB%d" % pbuf, t_vbs)
            op("vector", lambda e, pg=pg, tt=tt, pbuf=pbuf: e.scalar_tensor_tensor(
                out=gf_s[:], in0=pg, scalar=sg_a[:, tt:tt + 1],
                in1=gb_sl[:, pbuf * HB:(pbuf + 1) * HB], op0=ALU.mult, op1=ALU.add))
            t_ge = inc("vector", "SV")
            wait("scalar", "SV", t_ge)
            op("scalar", lambda e: e.activation(out=gf_s[:], in_=gf_s[:], func=AF.Silu))
            t_si = inc("scalar", "SA")
            wait("vector", "ST", mm_v)
            op("vector", lambda e, pv=pv, tt=tt, pbuf=pbuf: e.scalar_tensor_tensor(
                out=vf_s[:], in0=pv, scalar=sv_a[:, tt:tt + 1],
                in1=vb_sl[:, pbuf * HB:(pbuf + 1) * HB], op0=ALU.mult, op1=ALU.add))
            t_evac_v[tt % 2] = inc("vector", "SV")
            t_evac_a[tt % 2] = t_si
            wait("vector", "SA", t_si)
            if hb >= 1 and tt == 0:
                wait("vector", "GGV", t_gv_w)
            op("vector", lambda e, tt=tt: e.tensor_mul(out=gvst[:, tt * HB:(tt + 1) * HB],
                                                       in0=gf_s[:], in1=vf_s[:]))
            op("vector", lambda e, tt=tt: e.tensor_reduce(out=red1[:],
                                                          in_=gvst[:, tt * HB:(tt + 1) * HB],
                                                          axis=AX.X, op=ALU.max,
                                                          apply_absolute_value=True))
            if hb == 0:
                op("vector", lambda e, tt=tt: e.tensor_copy(out=gmax[:, tt:tt + 1], in_=red1[:]))
            else:
                op("vector", lambda e, tt=tt: e.tensor_max(out=gmax[:, tt:tt + 1],
                                                           in0=gmax[:, tt:tt + 1], in1=red1[:]))
            t_gvr = inc("vector", "SV")
            t_ev_last = t_gvr
        t_bsl_use[pbuf] = t_ev_last
        wait("gpsimd", "SV", t_gvr)
        t_gv_w = dma("gpsimd", "GGV",
                     sgv.ap().rearrange("(a p) h -> p a h", p=128)[:, :, hb * HB:(hb + 1) * HB],
                     gvst[:].rearrange("p (a h) -> p a h", h=HB))
        t_guse[pbuf] = mm_g
        t_vuse[pbuf] = mm_v

    if limit < 5:
        return finish(nc, prog, semh, sems)
    for j in range(NCHUNK):
        op("vector", lambda e, j=j: e.tensor_scalar_max(out=gmax[:, j:j + 1],
                                                        in0=gmax[:, j:j + 1], scalar1=1e-5))
        op("vector", lambda e, j=j: e.reciprocal(out=qs_a[:, j:j + 1], in_=gmax[:, j:j + 1]))
        op("vector", lambda e, j=j: e.tensor_scalar_mul(out=qs_a[:, j:j + 1], in0=qs_a[:, j:j + 1],
                                                        scalar1=127.0))
        op("vector", lambda e, j=j: e.tensor_mul(out=s2_a[:, j:j + 1], in0=gmax[:, j:j + 1],
                                                 in1=gwsc[:, 2:3]))
        op("vector", lambda e, j=j: e.tensor_scalar_mul(out=s2_a[:, j:j + 1], in0=s2_a[:, j:j + 1],
                                                        scalar1=1.0 / 127.0))
    t_qs = inc("vector", "SV")
    wait("sync", "GGV", t_gv_w)
    t_gvq_st = [0] * (2 * NCHUNK)
    for jh in range(2 * NCHUNK):
        j, hhalf = jh // 2, jh % 2
        sl = gvc[:, (jh % 2) * 4096:(jh % 2) * 4096 + 4096]
        if jh >= 2:
            wait("sync", "GGQ", sems["GGQ"])
        tok = dma("sync", "YV%d" % (jh % 2), sl,
                  sgv.ap()[j * CH:(j + 1) * CH, hhalf * 4096:(hhalf + 1) * 4096])
        wait("vector", "YV%d" % (jh % 2), tok)
        if jh == 0:
            wait("vector", "SV", t_qs)
        tfv = gq_s[:].bitcast(F32)
        op("vector", lambda e, sl=sl, j=j, tfv=tfv: e.tensor_scalar(
            out=tfv[:, 0:4096], in0=sl, scalar1=qs_a[:, j:j + 1], scalar2=RB + 0.5,
            op0=ALU.mult, op1=ALU.add))
        op("vector", lambda e, tfv=tfv: e.tensor_copy(out=gvqi[:], in_=tfv[:, 0:4096]))
        op("vector", lambda e, tfv=tfv: e.tensor_tensor(out=tfv[:, 4096:8192], in0=gvqi[:],
                                                        in1=tfv[:, 0:4096], op=ALU.is_gt))
        op("vector", lambda e, sl=sl, tfv=tfv: e.scalar_tensor_tensor(
            out=sl, in0=gvqi[:], scalar=-RB, in1=tfv[:, 4096:8192],
            op0=ALU.add, op1=ALU.subtract))
        t_gq2 = inc("vector", "SV")
        wait("gpsimd", "SV", t_gq2)
        t_gvq_st[jh] = dma("gpsimd", "GGQ",
                           sgvq.ap()[j * CH:(j + 1) * CH, hhalf * 4096:(hhalf + 1) * 4096], sl)
    t_gvq_w = t_gvq_st[-1]

    if limit < 6:
        return finish(nc, prog, semh, sems)
    wait("sync", "GGQ", t_gvq_w)
    t_ob_bc = dma("sync", "YOB", obb[:], dob.ap().partition_broadcast(128))

    t_oacc = {0: 0, 1: 0}
    t_oacc_fin = [0] * 8
    t_owu = [0, 0]
    t_gvTu = 0
    mm_o = 0
    ow_q_tok = [0]
    _nsh = 1 if limit < 6.35 else 8
    for sh in range(_nsh):
        sbuf_i = sh % 2
        owsl = owts[:, sbuf_i * 8 * D:(sbuf_i + 1) * 8 * D]
        if sh >= 2:
            wait("gpsimd", "ST", t_owu[sbuf_i])
        t_owl = 0
        if sh == 0:
            wait("gpsimd", "GGV", t_gv_w)
        for sub in range(8):
            wait("gpsimd", "SV", ow_q_tok[0])
            t_l = dma("gpsimd", "GOW", owl[:],
                      dowt.ap()[sh * 1024 + sub * 128: sh * 1024 + (sub + 1) * 128, :])
            wait("vector", "GOW", t_l)
            if sh >= 2 and sub == 0:
                wait("vector", "ST", t_owu[sbuf_i])
            op("vector", lambda e: e.tensor_scalar(
                out=x1l[:], in0=owl[:], scalar1=gwsc[:, 5:6], scalar2=0.5,
                op0=ALU.mult, op1=ALU.is_ge))
            op("vector", lambda e: e.tensor_scalar(
                out=owl[:], in0=owl[:], scalar1=negsc[:, 2:3], scalar2=0.5,
                op0=ALU.mult, op1=ALU.is_ge))
            op("vector", lambda e, owsl=owsl, sub=sub: e.tensor_sub(
                out=owsl[:, sub * D:(sub + 1) * D], in0=x1l[:], in1=owl[:]))
            t_owl = inc("vector", "SV")
            ow_q_tok[0] = t_owl
        if sh == 0:
            wait("sync", "GGQ", t_gvq_w)
        else:
            wait("sync", "ST", t_gvTu)
        t_gvl = 0
        for sub in range(8):
            hk = sh * 8 + sub
            t_gvl = dmaT("sync", "YGT", gvqT[:, sub * TC:(sub + 1) * TC],
                         sgvq.ap()[:, hk * 128:(hk + 1) * 128])
        wait("tensor", "SV", t_owl)
        wait("tensor", "YGT", t_gvl)
        for tt in range(8):
            ps = psA if tt % 2 == 0 else psB
            wait("tensor", "SV", t_oacc[tt % 2])
            for sub in range(8):
                for q in range(4):
                    cs = slice(q * 512, q * 512 + 512)
                    op("tensor", lambda e, sub=sub, tt=tt, cs=cs, ps=ps, owsl=owsl:
                       e.matmul(ps[:, cs],
                                gvqT[:, sub * TC + tt * CH: sub * TC + (tt + 1) * CH],
                                owsl[:, sub * D + cs.start:sub * D + cs.stop],
                                start=(sub == 0), stop=(sub == 7)))
            mm_o = inc("tensor", "ST")
            wait("vector", "ST", mm_o)
            if sh == 0:
                op("vector", lambda e, tt=tt, ps=ps: e.tensor_copy(
                    out=oacc[:, tt * D:(tt + 1) * D], in_=ps[:]))
            else:
                op("vector", lambda e, tt=tt, ps=ps: e.tensor_add(
                    out=oacc[:, tt * D:(tt + 1) * D], in0=oacc[:, tt * D:(tt + 1) * D], in1=ps[:]))
            t_oacc[tt % 2] = inc("vector", "SV")
            if sh == 7:
                t_oacc_fin[tt] = t_oacc[tt % 2]
        t_owu[sbuf_i] = mm_o
        t_gvTu = mm_o

    if limit < 6.5:
        return finish(nc, prog, semh, sems)
    if limit < 7:
        return finish(nc, prog, semh, sems)
    t_out_st = 0
    t_x1_use = 0
    t_out_use = 0
    first6 = True
    for j in range(NCHUNK):
        if j >= 1:
            wait("sync", "SV", t_x1_use)
            wait("sync", "GOUT", t_out_st)
        else:
            wait("sync", "SV", ow_q_tok[0])
        t_x1r = dma("sync", "YX1", x1l[:], sx1.ap()[j * CH:(j + 1) * CH, :])
        wait("vector", "SV", t_oacc_fin[j])
        wait("vector", "YX1", t_x1r)
        if first6:
            wait("vector", "YOB", t_ob_bc)
            first6 = False
        op("vector", lambda e: e.tensor_add(out=r2_sb[:], in0=x1l[:], in1=obb[:]))
        op("vector", lambda e, j=j: e.scalar_tensor_tensor(out=r2_sb[:],
                                                           in0=oacc[:, j * D:(j + 1) * D],
                                                           scalar=s2_a[:, j:j + 1], in1=r2_sb[:],
                                                           op0=ALU.mult, op1=ALU.add))
        ln_stats(r2_sb)
        if j >= 1:
            wait("vector", "GOUT", t_out_st)
        op("vector", lambda e: e.tensor_scalar(out=outst[:], in0=r2_sb[:], scalar1=mu_n[:, 0:1],
                                               scalar2=inv_s[:, 0:1], op0=ALU.add, op1=ALU.mult))
        t_oe = inc("vector", "SV")
        t_x1_use = t_oe
        wait("gpsimd", "SV", t_oe)
        t_out_st = dma("gpsimd", "GOUT", dout.ap()[j * CH:(j + 1) * CH, :], outst[:])

    wait("gpsimd", "GOUT", t_out_st)

    return _materialize(nc, prog, semh, sem_names)


def make_in_maps(x, wt, gwt, vwt, owt, lam, eta, gam, bet, gb, vb, ob):
    ones1 = np.ones((1, 128), dtype=np.float32)
    ii = np.arange(128, dtype=np.float32)
    dmat_h = ii[None, :] - ii[:, None]
    ltm_h = (dmat_h >= 0).astype(np.float32)
    dmat_h = np.where(dmat_h >= 0, dmat_h, 0.0).astype(np.float32)
    idn_h = np.eye(128, dtype=np.float32)
    iotap1 = np.broadcast_to(ii[None, :] + 1.0, (4, 128)).astype(np.float32).copy()
    iotarev = (127.0 - ii)[:, None].astype(np.float32).copy()
    c128 = np.full((4, 1), 128.0, dtype=np.float32)
    xprev = np.ascontiguousarray(x[0:PT])

    in_maps = []
    for c in range(NCORE):
        t0 = c * TC
        s_idx = ii[:, None] + 128.0 * np.arange(NPT, dtype=np.float32)[None, :]
        pmask = (s_idx < t0).astype(np.float32)
        pexp = np.where(pmask > 0, (t0 - 1.0) - s_idx, 0.0).astype(np.float32)
        in_maps.append({
            "xc": np.ascontiguousarray(x[t0:t0 + TC]),
            "xprev": xprev,
            "wt": wt, "gwt": gwt, "vwt": vwt, "owt": owt,
            "lam_logit": lam, "eta": eta, "gam": gam, "bet": bet,
            "gb": gb, "vb": vb, "ob": ob,
            "ones1": ones1, "dmat": dmat_h, "ltm": ltm_h, "idn": idn_h,
            "iotap1": iotap1, "iotarev": iotarev, "c128": c128,
            "pexp": np.ascontiguousarray(pexp), "pmask": np.ascontiguousarray(pmask),
        })
    return in_maps


def kernel(L0_attn_x, L0_attn_W, L0_attn_lam_logit, L0_attn_eta, L0_attn_gamma,
           L0_attn_beta, L0_ffn_gate_w, L0_ffn_gate_b, L0_ffn_val_w, L0_ffn_val_b,
           L0_ffn_out_w, L0_ffn_out_b):
    x = np.ascontiguousarray(np.asarray(L0_attn_x, dtype=np.float32).reshape(T, D))
    wt = np.ascontiguousarray(np.asarray(L0_attn_W, dtype=np.float32).T)
    gwt = np.ascontiguousarray(np.asarray(L0_ffn_gate_w, dtype=np.float32).T)
    vwt = np.ascontiguousarray(np.asarray(L0_ffn_val_w, dtype=np.float32).T)
    owt = np.ascontiguousarray(np.asarray(L0_ffn_out_w, dtype=np.float32).T)
    lam = np.asarray(L0_attn_lam_logit, dtype=np.float32).reshape(1, 4)
    eta = np.asarray(L0_attn_eta, dtype=np.float32).reshape(1, 4)
    gam = np.asarray(L0_attn_gamma, dtype=np.float32).reshape(1, D)
    bet = np.asarray(L0_attn_beta, dtype=np.float32).reshape(1, D)
    gb = np.asarray(L0_ffn_gate_b, dtype=np.float32).reshape(1, H)
    vb = np.asarray(L0_ffn_val_b, dtype=np.float32).reshape(1, H)
    ob = np.asarray(L0_ffn_out_b, dtype=np.float32).reshape(1, D)

    in_maps = make_in_maps(x, wt, gwt, vwt, owt, lam, eta, gam, bet, gb, vb, ob)
    nc = build_program()
    res = run_bass_kernel_spmd(nc, in_maps, core_ids=list(range(NCORE)))
    out = np.concatenate([np.asarray(r["out"]) for r in res.results], axis=0)
    return out.reshape(1, T, D).astype(np.float32)
